# version 16
# baseline (speedup 1.0000x reference)
import numpy as np
import ml_dtypes
import concourse.bass as bass
import concourse.mybir as mybir
from concourse.bass_utils import run_bass_kernel_spmd
from concourse.alu_op_type import AluOpType as ALU

F32 = mybir.dt.float32
BF16 = mybir.dt.bfloat16
I8 = mybir.dt.int8
AF = mybir.ActivationFunctionType
AX = mybir.AxisListType
BF = ml_dtypes.bfloat16

B, T, HID = 16, 2048, 2048
NH, NKV, DH = 16, 4, 128
INTER = 8192
NCORE = 8
NSEQ = 2
SC = 512
NTIL = 4
NSCQ = T // SC
NCH = HID // 128
NIB = INTER // 128
SCALE = 1.0 / np.sqrt(DH)

XNT = 0
R1U = 8192
R1L = 16384
KV = 24576
P_ = 32768
Q_ = 40960
FU = 49152
IDM = 81920
ONESC = 82048
MKD = 82176
MKO = 82304
COSB = 82432
SNB = 82944
A16N = 83456

XN0, XN1 = Q_ + 4096, Q_ + 6144
QSB = FU + 4096
QSQ = FU + 6144
QS_ = FU + 8192
RA_ = FU + 10240
RB_ = FU + 12288
QN_ = FU + 14336
AEX = (FU + 16384, FU + 17664)
RCP = (FU + 18944, FU + 19200)

WGB = (R1U + 0, R1U + 2048)
WUB = (R1U + 4096, R1U + 6144)
WDB = (R1L + 0, R1L + 2048, R1U + 4096, R1U + 6144)
SILT = (R1L + 4096, R1L + 4608)
UPT = (R1L + 5120, R1L + 5632)
GSIL = (R1L + 6144, R1L + 7168)

XA = (0, 2048)
GSC = 0
STAT = 4096
QSS = 4224
EPSC = 4272
LNR = 4288
SDN = 4544
SGU = 6592
XF = (6720, 7744)
OUTF = (8768, 9792)
A32N = 10816


def build_program(taps=(), sc_barrier=False, e2_serial=False):
    nc = bass.Bass("TRN2", target_bir_lowering=False)
    x_d = nc.dram_tensor("x", [NSEQ, T, HID], F32, kind="ExternalInput").ap()
    wq_d = nc.dram_tensor("wq", [HID, NH * DH], F32, kind="ExternalInput").ap()
    wk_d = nc.dram_tensor("wk", [HID, NKV * DH], F32, kind="ExternalInput").ap()
    wv_d = nc.dram_tensor("wv", [HID, NKV * DH], F32, kind="ExternalInput").ap()
    wo_d = nc.dram_tensor("wo", [NH * DH, HID], F32, kind="ExternalInput").ap()
    wg_d = nc.dram_tensor("wg", [NIB, 128, HID], I8, kind="ExternalInput").ap()
    wu_d = nc.dram_tensor("wu", [NIB, 128, HID], I8, kind="ExternalInput").ap()
    wd_d = nc.dram_tensor("wd", [NIB, 128, HID], I8, kind="ExternalInput").ap()
    sg_d = nc.dram_tensor("sg", [128, NIB], F32, kind="ExternalInput").ap()
    su_d = nc.dram_tensor("su", [128, NIB], F32, kind="ExternalInput").ap()
    sd_d = nc.dram_tensor("sd", [1, HID], F32, kind="ExternalInput").ap()
    eps_d = nc.dram_tensor("eps", [1, 2], F32, kind="ExternalInput").ap()
    idm_d = nc.dram_tensor("idm", [128, 128], BF16, kind="ExternalInput").ap()
    on_d = nc.dram_tensor("ones", [128, 128], BF16, kind="ExternalInput").ap()
    mkd_d = nc.dram_tensor("mkd", [128, 128], BF16, kind="ExternalInput").ap()
    mko_d = nc.dram_tensor("mko", [128, 128], BF16, kind="ExternalInput").ap()
    cos_d = nc.dram_tensor("cosp", [128, T], BF16, kind="ExternalInput").ap()
    sn_d = nc.dram_tensor("snp", [128, T], BF16, kind="ExternalInput").ap()
    out_d = nc.dram_tensor("out", [NSEQ, T, HID], F32, kind="ExternalOutput").ap()
    wqb_d = nc.dram_tensor("wqb", [HID, NH * DH], BF16, kind="Internal").ap()
    wkb_d = nc.dram_tensor("wkb", [HID, NKV * DH], BF16, kind="Internal").ap()
    wvb_d = nc.dram_tensor("wvb", [HID, NKV * DH], BF16, kind="Internal").ap()
    wob_d = nc.dram_tensor("wob", [NH * DH, HID], BF16, kind="Internal").ap()

    tap_d = {}
    for tp, size in (("xnt", 8192), ("kv", 8192), ("qt", 8192), ("ot", 8192),
                     ("ssb", 8192), ("fuse", 32768)):
        if tp in taps:
            tap_d[tp] = nc.dram_tensor(
                "tap_" + tp, [NSEQ * NSCQ, 128, size], BF16, kind="ExternalOutput").ap()
    if "gsc" in taps:
        tap_d["gsc"] = nc.dram_tensor(
            "tap_gsc", [NSEQ * NSCQ, 8, 128, 1024], F32, kind="ExternalOutput").ap()

    import contextlib
    _es = contextlib.ExitStack()
    _slotnames = ["slU", "slL", "cos", "wgs0", "wgs1", "wus0", "wus1",
                  "wds0", "wds1", "wds2", "wds3", "xa0", "xa1", "xf0", "xf1", "st0", "st1"]
    with (
        nc.sbuf_tensor("a16", [128, A16N], BF16) as a16_t,
        nc.sbuf_tensor("a32", [128, A32N], F32) as a32_t,
        nc.psum_tensor("ps", [128, 4096], F32) as ps_t,
        nc.semaphore("sWG") as hWG,
        nc.semaphore("sXL") as hXL,
        nc.semaphore("sST") as hST,
        nc.semaphore("sPE") as hPE,
        nc.semaphore("sACT") as hACT,
        nc.semaphore("sDVE") as hDVE,
        nc.Block() as block,
        _es,
    ):
        _slot_h = {n: _es.enter_context(nc.semaphore("s_" + n)) for n in _slotnames}
        PS = ps_t[:, :]
        PSB = PS.bitcast(BF16)
        def gsc_buf(ts):
            return b16(Q_ + ts * 1024, 1024)

        sems = {"wg": hWG, "xl": hXL, "st": hST, "pe": hPE, "act": hACT, "dve": hDVE}
        sems.update(_slot_h)
        cnt = {k: 0 for k in sems}

        def inc(instr, name, step=1):
            cnt[name] += step
            instr.then_inc(sems[name], step)
            return cnt[name]

        def waits(eng, ws):
            for name, tgt in ws:
                if tgt > 0:
                    eng.wait_ge(sems[name], tgt)

        def b16(off, n):
            return a16_t[:, off:off + n]

        def b32(off, n):
            return a32_t[:, off:off + n]

        ident = b16(IDM, 128)
        ones128 = b16(ONESC, 128)

        def _(g):
            ii = g.dma_start(out=b16(IDM, 128), in_=idm_d).then_inc(hWG, 16)
            ii = g.dma_start(out=b16(ONESC, 128), in_=on_d).then_inc(hWG, 16)
            ii = g.dma_start(out=b16(MKD, 128), in_=mkd_d).then_inc(hWG, 16)
            ii = g.dma_start(out=b16(MKO, 128), in_=mko_d).then_inc(hWG, 16)
            ii = g.dma_start(out=b32(SDN, 2048),
                        in_=sd_d[0:1, :].broadcast_to([128, HID])).then_inc(hWG, 16)
            ii = g.dma_start(out=b32(SGU, 64), in_=sg_d).then_inc(hWG, 16)
            ii = g.dma_start(out=b32(SGU + 64, 64), in_=su_d).then_inc(hWG, 16)
            ii = g.dma_start(out=b32(EPSC, 2),
                             in_=eps_d[0:1, :].broadcast_to([128, 2])).then_inc(hWG, 16)
        block.gpsimd(_)
        cnt["wg"] += 8 * 16
        consts = cnt["wg"]

        wcv = [(wq_d, wqb_d, NH * DH), (wk_d, wkb_d, NKV * DH),
               (wv_d, wvb_d, NKV * DH), (wo_d, wob_d, HID)]
        pro_st = None
        for wi, (srcw, dstw, ow) in enumerate(wcv):
            def _(g, wi=wi, srcw=srcw, ow=ow):
                if wi >= 1:
                    waits(g, [("st", cnt["st"])])
                ii = g.dma_start(
                    out=b16(FU, 16 * ow).rearrange("p (c n) -> p c n", n=ow),
                    in_=srcw.rearrange("(c p) n -> p c n", p=128))
                inc(ii, "wg", 16)
            block.gpsimd(_)
            def _(sy, wi=wi, dstw=dstw, ow=ow):
                waits(sy, [("wg", cnt["wg"])])
                ii = sy.dma_start(
                    out=dstw.rearrange("(c p) n -> p c n", p=128),
                    in_=b16(FU, 16 * ow).rearrange("p (c n) -> p c n", n=ow))
                inc(ii, "st", 16)
            block.sync(_)
        pro_st = cnt["st"]

        bnd = {"pe_E2": 0, "fu_dve": 0, "dve_F": 0, "act_F": 0,
               "rope_dve": 0, "pe_D3": 0, "otv_last": 0, "a_tr_last": 0,
               "bank03_dve": 0, "bank47_dve": 0}
        last_xf_dve = {0: 0, 1: 0}
        last_out_st = {0: ("st0", 0), 1: ("st1", 0)}

        apre = {}

        def emit_apre(scn2, gates=()):
            si2, u2 = divmod(scn2, NSCQ)
            base2 = u2 * SC
            marks = {"xn": [0] * NTIL, "xl": [None] * NTIL, "stats": [0] * NTIL}
            apre[scn2] = marks
            prev = apre.get(scn2 - 1)
            for tt in range(NTIL):
                xa_off = XA[tt % 2]
                xn_off = (XN0, XN1)[tt % 2]
                st = STAT + tt * 32

                def _(sy, tt=tt, xa_off=xa_off):
                    ws = []
                    if prev is not None:
                        ws.append(("act", prev["xn"][(tt % 2) + 2]))
                    if tt >= 2:
                        ws.append(("act", marks["xn"][tt - 2]))
                    waits(sy, ws)
                    ii = sy.dma_start(out=b32(xa_off, 2048),
                                 in_=x_d[si2, base2 + tt * 128: base2 + (tt + 1) * 128, :])
                    marks["xl"][tt] = ("xa%d" % (tt % 2), inc(ii, "xa%d" % (tt % 2), 16))
                block.sync(_)

                def _(v, tt=tt, xa_off=xa_off, st=st):
                    waits(v, [marks["xl"][tt]])
                    ii = None
                    for i in range(4):
                        ii = v.bn_stats(b32(st, 24)[:, i * 6:(i + 1) * 6],
                                        b32(xa_off, 2048)[:, i * 512:(i + 1) * 512])
                    inc(ii, "dve")
                    v.wait_ge(hDVE, cnt["dve"])
                    ii = v.bn_aggr(b32(st + 24, 2), b32(st, 24))
                    inc(ii, "dve")
                block.vector(_)

                def _(s, tt=tt, st=st):
                    waits(s, [("dve", cnt["dve"]), ("wg", consts)])
                    s.activation(b32(st + 26, 1), b32(st + 24, 2)[:, 1:2], AF.Ln,
                                 bias=b32(EPSC, 1))
                    ii = s.activation(b32(st + 27, 1), b32(st + 26, 1), AF.Exp,
                                      scale=-0.5)
                    inc(ii, "act")
                block.scalar(_)

                def _(v, tt=tt, st=st):
                    waits(v, [("act", cnt["act"])])
                    ii = v.scalar_tensor_tensor(b32(st + 28, 1),
                                                b32(st + 24, 2)[:, 0:1], -1.0,
                                                b32(st + 27, 1), ALU.mult, ALU.mult)
                    marks["stats"][tt] = inc(ii, "dve")
                block.vector(_)

                if tt < 2:
                    def _(s, tt=tt, xa_off=xa_off, xn_off=xn_off, st=st,
                          gates=tuple(gates)):
                        ws = [("dve", marks["stats"][tt])]
                        ws.extend(gates)
                        waits(s, ws)
                        ii = s.activation(b16(xn_off, 2048), b32(xa_off, 2048),
                                          AF.Identity,
                                          scale=b32(st + 27, 1), bias=b32(st + 28, 1))
                        marks["xn"][tt] = inc(ii, "act")
                    block.scalar(_)

        emit_apre(0)

        for scn in range(NSEQ * NSCQ):
            si, u = divmod(scn, NSCQ)
            pb = u % 2
            base = u * SC
            slabm = {}
            amk = apre[scn]
            xn_act = amk["xn"]

            def _(sq):
                if scn == 0:
                    waits(sq, [("st", pro_st)])
                waits(sq, [("pe", bnd["pe_E2"]), ("dve", bnd["fu_dve"])])
                ii = sq.dma_start(out=b16(R1U, 8192).rearrange("p (c n) -> p c n", n=512),
                            in_=wkb_d.rearrange("(c p) n -> p c n", p=128))
                slabm["wk"] = ("slU", inc(ii, "slU", 16))
                waits(sq, [("dve", bnd["dve_F"]), ("pe", bnd["pe_E2"])])
                ii = sq.dma_start(out=b16(R1L, 8192).rearrange("p (c n) -> p c n", n=512),
                            in_=wvb_d.rearrange("(c p) n -> p c n", p=128))
                slabm["wv"] = ("slL", inc(ii, "slL", 16))
            block.sync(_)

            def _(g):
                waits(g, [("dve", bnd["rope_dve"])])
                ii = g.dma_start(out=b16(COSB, 512), in_=cos_d[:, u * 512:(u + 1) * 512])
                inc(ii, "cos", 16)
                ii = g.dma_start(out=b16(SNB, 512), in_=sn_d[:, u * 512:(u + 1) * 512])
                slabm["cos"] = ("cos", inc(ii, "cos", 16))
            block.gpsimd(_)

            _scope_id, _ = nc.enter_named_scope("sc%d_A" % scn, False)
            a_tr = [0] * NTIL
            a_evac = [0] * NTIL
            xn_act = [0] * NTIL
            xl_mark = [0] * NTIL
            k_mm = [0] * NTIL
            v_mm = [0] * NTIL
            kt_tr = [0] * NTIL
            kT_ev = [0] * NTIL
            v_ev = [0] * NTIL
            kn_dve = [0] * NTIL
            k_ev_act = [0] * NTIL
            kbank = [4, 6, 4, 6]
            vbank = [5, 7, 5, 7]

            def emit_ktr(tt):
                def _(t, tt=tt):
                    ws = [("dve", kn_dve[tt])]
                    if tt >= 2:
                        ws.append(("dve", kT_ev[tt - 2]))
                    waits(t, ws)
                    slot = (tt % 2) * 2048
                    ii = None
                    for h in range(NKV):
                        ii = t.transpose(PSB[:, slot + h * 128: slot + (h + 1) * 128],
                                         b16(QN_ + (tt % 2) * 512 + h * 128, 128), ident)
                    kt_tr[tt] = inc(ii, "pe")
                block.tensor(_)

                def _(v, tt=tt):
                    waits(v, [("pe", kt_tr[tt])])
                    slot = (tt % 2) * 2048
                    ii = v.tensor_copy(
                        b16(KV + pb * 4096, 2048).rearrange("p (h t) -> p h t", t=512)
                        [:, :, tt * 128:(tt + 1) * 128],
                        PSB[:, slot:slot + 512].rearrange("p (h t) -> p h t", t=128))
                    kT_ev[tt] = inc(ii, "dve")
                block.vector(_)

            for tt in range(NTIL):
                xn_off = (XN0, XN1, Q_ + 0, Q_ + 2048)[tt]
                if tt >= 2:
                    def _(s, tt=tt, xn_off=xn_off):
                        st = STAT + tt * 32
                        waits(s, [("dve", amk["stats"][tt]), ("dve", bnd["dve_F"])])
                        ii = s.activation(b16(xn_off, 2048), b32(XA[tt % 2], 2048),
                                          AF.Identity,
                                          scale=b32(st + 27, 1), bias=b32(st + 28, 1))
                        amk["xn"][tt] = inc(ii, "act")
                    block.scalar(_)

                def _(t, tt=tt, xn_off=xn_off):
                    ws = [("act", amk["xn"][tt]), ("wg", consts)]
                    if tt >= 2:
                        ws.append(("dve", a_evac[tt - 2]))
                        ws.append(("dve", kT_ev[tt - 2]))
                    else:
                        ws.append(("dve", bnd["bank03_dve"]))
                    waits(t, ws)
                    slot = (tt % 2) * 2048
                    ii = None
                    for c in range(NCH):
                        ii = t.transpose(PSB[:, slot + c * 128: slot + (c + 1) * 128],
                                         b16(xn_off, 2048)[:, c * 128:(c + 1) * 128],
                                         ident)
                    a_tr[tt] = inc(ii, "pe")
                block.tensor(_)

                def _(v, tt=tt):
                    waits(v, [("pe", a_tr[tt])])
                    slot = (tt % 2) * 2048
                    ii = v.tensor_copy(
                        b16(XNT, 8192).rearrange("p (c t) -> p c t", t=512)
                        [:, :, tt * 128:(tt + 1) * 128],
                        PSB[:, slot:slot + 2048].rearrange("p (c t) -> p c t", t=128))
                    a_evac[tt] = inc(ii, "dve")
                block.vector(_)

                def _(t, tt=tt):
                    ws = [("dve", a_evac[tt]), slabm["wk"]]
                    if tt >= 2:
                        ws.append(("act", k_ev_act[tt - 2]))
                    else:
                        ws.append(("dve", bnd["bank47_dve"]))
                    waits(t, ws)
                    kb = PS[:, kbank[tt] * 512:(kbank[tt] + 1) * 512]
                    ii = None
                    for c in range(NCH):
                        ii = t.matmul(kb, b16(XNT + c * 512 + tt * 128, 128),
                                      b16(R1U + c * 512, 512),
                                      start=(c == 0), stop=(c == NCH - 1))
                    k_mm[tt] = inc(ii, "pe")
                block.tensor(_)

                def _(s, tt=tt):
                    ws = [("pe", k_mm[tt])]
                    if tt >= 2:
                        ws.append(("dve", kn_dve[tt - 2]))
                    waits(s, ws)
                    kb = PS[:, kbank[tt] * 512:(kbank[tt] + 1) * 512]
                    ii = s.activation(b16(QSB + (tt % 2) * 512, 512), kb, AF.Copy)
                    k_ev_act[tt] = inc(ii, "act")
                block.scalar(_)

                def _(v, tt=tt):
                    waits(v, [("act", k_ev_act[tt])])
                    v.tensor_tensor(b16(QSQ, 512), b16(QSB + (tt % 2) * 512, 512),
                                    b16(QSB + (tt % 2) * 512, 512), ALU.mult)
                    ii = v.tensor_reduce(
                        b32(QSS, 4),
                        b16(QSQ, 512).rearrange("p (h d) -> p h d", d=128),
                        AX.X, ALU.add)
                    inc(ii, "dve")
                block.vector(_)

                def _(s, tt=tt):
                    waits(s, [("dve", cnt["dve"]), ("wg", consts)])
                    s.activation(b32(QSS + 8, 4), b32(QSS, 4), AF.Ln, bias=b32(EPSC + 1, 1))
                    ii = s.activation(b32(QSS + 12, 4), b32(QSS + 8, 4), AF.Exp,
                                      scale=-0.5)
                    inc(ii, "act")
                block.scalar(_)

                def _(v, tt=tt):
                    ws = [("act", cnt["act"]), slabm["cos"]]
                    if tt >= 2:
                        ws.append(("pe", kt_tr[tt - 2]))
                    waits(v, ws)
                    ks3 = b16(QS_, 512).rearrange("p (h d) -> p h d", d=128)
                    v.tensor_tensor(ks3,
                                    b16(QSB + (tt % 2) * 512, 512).rearrange("p (h d) -> p h d", d=128),
                                    b32(QSS + 12, 4).unsqueeze(2).broadcast_to([128, 4, 128]),
                                    ALU.mult)
                    cosr = b16(COSB + tt * 128, 128).unsqueeze(1).broadcast_to([128, 4, 128])
                    v.tensor_tensor(b16(RA_, 512).rearrange("p (h d) -> p h d", d=128),
                                    ks3, cosr, ALU.mult)
                    sn1 = b16(SNB + tt * 128, 64).unsqueeze(1).broadcast_to([128, 4, 64])
                    sn2 = b16(SNB + tt * 128 + 64, 64).unsqueeze(1).broadcast_to([128, 4, 64])
                    rb3 = b16(RB_, 512).rearrange("p (h d) -> p h d", d=128)
                    v.tensor_tensor(rb3[:, :, 0:64], ks3[:, :, 64:128], sn1, ALU.mult)
                    v.tensor_tensor(rb3[:, :, 64:128], ks3[:, :, 0:64], sn2, ALU.mult)
                    ii = v.tensor_tensor(b16(QN_ + (tt % 2) * 512, 512), b16(RA_, 512),
                                         b16(RB_, 512), ALU.add)
                    kn_dve[tt] = inc(ii, "dve")
                block.vector(_)

                def _(t, tt=tt):
                    ws = [slabm["wv"]]
                    if tt >= 2:
                        ws.append(("dve", v_ev[tt - 2]))
                    waits(t, ws)
                    vb = PS[:, vbank[tt] * 512:(vbank[tt] + 1) * 512]
                    ii = None
                    for c in range(NCH):
                        ii = t.matmul(vb, b16(XNT + c * 512 + tt * 128, 128),
                                      b16(R1L + c * 512, 512),
                                      start=(c == 0), stop=(c == NCH - 1))
                    v_mm[tt] = inc(ii, "pe")
                block.tensor(_)

                def _(v, tt=tt):
                    waits(v, [("pe", v_mm[tt])])
                    vb = PS[:, vbank[tt] * 512:(vbank[tt] + 1) * 512]
                    ii = v.tensor_copy(b16(KV + pb * 4096 + 2048 + tt * 512, 512), vb)
                    v_ev[tt] = inc(ii, "dve")
                block.vector(_)

                if tt >= 1:
                    emit_ktr(tt - 1)

            emit_ktr(NTIL - 1)
            nc.leave_named_scope("sc%d_A" % scn, _scope_id, False)
            _scope_id, _ = nc.enter_named_scope("sc%d_B" % scn, False)

            if "xnt" in tap_d:
                def _(sy):
                    waits(sy, [("dve", a_evac[NTIL - 1])])
                    ii = sy.dma_start(out=tap_d["xnt"][scn], in_=b16(XNT, 8192))
                    inc(ii, "st", 16)
                block.sync(_)

            if "kv" in tap_d:
                def _(sy):
                    waits(sy, [("dve", kT_ev[NTIL - 1]), ("dve", v_ev[NTIL - 1])])
                    ii = sy.dma_start(out=tap_d["kv"][scn], in_=b16(KV, 8192))
                    inc(ii, "st", 16)
                block.sync(_)

            nc.leave_named_scope("sc%d_B" % scn, _scope_id, False)
            _scope_id, _ = nc.enter_named_scope("sc%d_C0" % scn, False)
            q_mm = {}
            q_ev_act = {}
            qt_tr = {}
            qT_ev = {}
            qn_dve = {}

            def emit_wq_load(qr, gate):
                def _(sq, qr=qr, gate=gate):
                    waits(sq, gate)
                    buf = R1U if qr % 2 == 0 else R1L
                    sl = "slU" if qr % 2 == 0 else "slL"
                    ii = sq.dma_start(out=b16(buf, 8192).rearrange("p (c n) -> p c n", n=512),
                                in_=wqb_d[:, qr * 512:(qr + 1) * 512].rearrange(
                                    "(c p) n -> p c n", p=128))
                    slabm["wq%d" % qr] = (sl, inc(ii, sl, 16))
                block.sync(_)

            emit_wq_load(0, [("pe", k_mm[NTIL - 1])])
            emit_wq_load(1, [("pe", v_mm[NTIL - 1])])

            def emit_qtr(un):
                qr2, tt2 = divmod(un, NTIL)

                def _(t, un=un):
                    ws = [("dve", qn_dve[un])]
                    if un >= 2:
                        ws.append(("dve", qT_ev[un - 2]))
                    else:
                        ws.append(("dve", v_ev[NTIL - 1]))
                        ws.append(("act", k_ev_act[NTIL - 1]))
                    waits(t, ws)
                    slot = 4096 + (un % 2) * 1024
                    ii = None
                    for h in range(4):
                        ii = t.transpose(PSB[:, slot + h * 128: slot + (h + 1) * 128],
                                         b16(QN_ + (un % 4) * 512 + h * 128, 128), ident)
                    qt_tr[un] = inc(ii, "pe")
                block.tensor(_)

                def _(v, qr2=qr2, tt2=tt2, un=un):
                    waits(v, [("pe", qt_tr[un])])
                    slot = 4096 + (un % 2) * 1024
                    ii = v.tensor_copy(
                        b16(P_, 8192).rearrange("p (h t) -> p h t", t=512)
                        [:, qr2 * 4:(qr2 + 1) * 4, tt2 * 128:(tt2 + 1) * 128],
                        PSB[:, slot:slot + 512].rearrange("p (h t) -> p h t", t=128))
                    qT_ev[un] = inc(ii, "dve")
                block.vector(_)

            for qr in range(4):
                for tt in range(NTIL):
                    un = qr * NTIL + tt
                    qbank = un % 4
                    if un >= 3:
                        emit_qtr(un - 3)

                    def _(t, qr=qr, tt=tt, un=un, qbank=qbank):
                        ws = [slabm["wq%d" % qr], ("dve", a_evac[tt])]
                        if un >= 4:
                            ws.append(("act", q_ev_act[un - 4]))
                        else:
                            ws.append(("dve", kT_ev[NTIL - 1]))
                        waits(t, ws)
                        buf = R1U if qr % 2 == 0 else R1L
                        qb = PS[:, qbank * 512:(qbank + 1) * 512]
                        ii = None
                        for c in range(NCH):
                            ii = t.matmul(qb, b16(XNT + c * 512 + tt * 128, 128),
                                          b16(buf + c * 512, 512),
                                          start=(c == 0), stop=(c == NCH - 1))
                        q_mm[un] = inc(ii, "pe")
                    block.tensor(_)

                    def _(s, un=un, qbank=qbank):
                        ws = [("pe", q_mm[un])]
                        if un >= 2:
                            ws.append(("dve", qn_dve[un - 2]))
                        waits(s, ws)
                        qb = PS[:, qbank * 512:(qbank + 1) * 512]
                        ii = s.activation(b16(QSB + (un % 2) * 512, 512), qb, AF.Copy)
                        q_ev_act[un] = inc(ii, "act")
                    block.scalar(_)

                    def _(v, un=un):
                        waits(v, [("act", q_ev_act[un])])
                        v.tensor_tensor(b16(QSQ, 512), b16(QSB + (un % 2) * 512, 512),
                                        b16(QSB + (un % 2) * 512, 512), ALU.mult)
                        ii = v.tensor_reduce(
                            b32(QSS + 16, 4),
                            b16(QSQ, 512).rearrange("p (h d) -> p h d", d=128),
                            AX.X, ALU.add)
                        inc(ii, "dve")
                    block.vector(_)

                    def _(s, un=un):
                        waits(s, [("dve", cnt["dve"]), ("wg", consts)])
                        s.activation(b32(QSS + 24, 4), b32(QSS + 16, 4), AF.Ln,
                                     bias=b32(EPSC + 1, 1))
                        ii = s.activation(b32(QSS + 28, 4), b32(QSS + 24, 4), AF.Exp,
                                          scale=-0.5)
                        inc(ii, "act")
                    block.scalar(_)

                    def _(v, tt=tt, un=un):
                        ws = [("act", cnt["act"]), slabm["cos"]]
                        if un >= 4:
                            ws.append(("pe", qt_tr[un - 4]))
                        waits(v, ws)
                        ks3 = b16(QS_, 512).rearrange("p (h d) -> p h d", d=128)
                        v.tensor_tensor(ks3,
                                        b16(QSB + (un % 2) * 512, 512).rearrange("p (h d) -> p h d", d=128),
                                        b32(QSS + 28, 4).unsqueeze(2).broadcast_to([128, 4, 128]),
                                        ALU.mult)
                        cosr = b16(COSB + tt * 128, 128).unsqueeze(1).broadcast_to([128, 4, 128])
                        v.tensor_tensor(b16(RA_, 512).rearrange("p (h d) -> p h d", d=128),
                                        ks3, cosr, ALU.mult)
                        sn1 = b16(SNB + tt * 128, 64).unsqueeze(1).broadcast_to([128, 4, 64])
                        sn2 = b16(SNB + tt * 128 + 64, 64).unsqueeze(1).broadcast_to([128, 4, 64])
                        rb3 = b16(RB_, 512).rearrange("p (h d) -> p h d", d=128)
                        v.tensor_tensor(rb3[:, :, 0:64], ks3[:, :, 64:128], sn1, ALU.mult)
                        v.tensor_tensor(rb3[:, :, 64:128], ks3[:, :, 0:64], sn2, ALU.mult)
                        ii = v.tensor_tensor(b16(QN_ + (un % 4) * 512, 512), b16(RA_, 512),
                                             b16(RB_, 512), ALU.add)
                        qn_dve[un] = inc(ii, "dve")
                    block.vector(_)

                if qr == 0:
                    emit_wq_load(2, [("pe", q_mm[NTIL - 1])])
                elif qr == 1:
                    emit_wq_load(3, [("pe", q_mm[NTIL + NTIL - 1])])

            emit_qtr(4 * NTIL - 3)
            emit_qtr(4 * NTIL - 2)
            emit_qtr(4 * NTIL - 1)

            def emit_wo_load(qr, gate):
                def _(sq, qr=qr, gate=gate):
                    waits(sq, gate)
                    buf = R1U if qr % 2 == 0 else R1L
                    sl = "slU" if qr % 2 == 0 else "slL"
                    ii = sq.dma_start(out=b16(buf, 8192).rearrange("p (c n) -> p c n", n=512),
                                in_=wob_d[:, qr * 512:(qr + 1) * 512].rearrange(
                                    "(c p) n -> p c n", p=128))
                    slabm["wo%d" % qr] = (sl, inc(ii, sl, 16))
                block.sync(_)

            emit_wo_load(0, [("pe", q_mm[2 * NTIL + NTIL - 1])])
            emit_wo_load(1, [("pe", q_mm[3 * NTIL + NTIL - 1])])

            if "qt" in tap_d:
                def _(sy):
                    waits(sy, [("dve", qT_ev[15])])
                    ii = sy.dma_start(out=tap_d["qt"][scn], in_=b16(P_, 8192))
                    inc(ii, "st", 16)
                block.sync(_)

            nc.leave_named_scope("sc%d_C0" % scn, _scope_id, False)
            _scope_id, _ = nc.enter_named_scope("sc%d_C1" % scn, False)
            sp_mm = {}
            exp_g = {}
            mask_g = {}
            ot_mm = {}
            recip_g = {}
            otv_g = {}
            ngroups = NTIL * NKV * 2

            APQ = FU + 4096
            RCPW = FU + 16384
            LNT = XF[0]
            rs_mm = {}
            rcp_a = {}
            otm = {}
            otvm = {}

            for qt in range(NTIL):
                gt = u * NTIL + qt
                gcs = list(range(max(0, gt - 4), gt + 1))
                nch_ = len(gcs)
                for kvh in range(NKV):
                    for pr in range(2):
                        g_ = qt * 8 + kvh * 2 + pr
                        h0 = kvh * 4 + pr * 2
                        buf = g_ % 2
                        spb = 0 if buf == 0 else 1536

                        def _(t, qt=qt, gcs=gcs, nch_=nch_, kvh=kvh, h0=h0,
                              g_=g_, spb=spb):
                            ws = [("dve", qT_ev[(h0 // 4) * NTIL + qt]),
                                  ("dve", kT_ev[NTIL - 1]), ("dve", v_ev[NTIL - 1])]
                            if g_ >= 2:
                                ws.append(("act", exp_g[g_ - 2]))
                            else:
                                ws.append(("act", q_ev_act[15]))
                                ws.append(("dve", qT_ev[15]))
                            waits(t, ws)
                            ii = None
                            for ci, gc in enumerate(gcs):
                                pbc = (gc // NTIL) % 2
                                kts = b16(KV + pbc * 4096 + kvh * 512 + (gc % NTIL) * 128, 128)
                                for hs in range(2):
                                    h = h0 + hs
                                    ii = t.matmul(
                                        PS[:, spb + hs * 640 + ci * 128:
                                           spb + hs * 640 + (ci + 1) * 128],
                                        kts, b16(P_ + h * 512 + qt * 128, 128),
                                        start=True, stop=True, skip_group_check=True)
                            sp_mm[g_] = inc(ii, "pe")
                        block.tensor(_)

                        def _(s, nch_=nch_, g_=g_, spb=spb, h0=h0):
                            ws = [("pe", sp_mm[g_])]
                            if g_ >= 2:
                                ws.append(("dve", mask_g[g_ - 2]))
                            waits(s, ws)
                            ii = s.activation(b16(APQ + h0 * 640, 1280),
                                              PS[:, spb: spb + 1280],
                                              AF.Exp, scale=float(SCALE))
                            exp_g[g_] = inc(ii, "act")
                        block.scalar(_)

                        def _(v, gt=gt, nch_=nch_, g_=g_, h0=h0):
                            waits(v, [("act", exp_g[g_]), ("wg", consts)])
                            both_d = b16(APQ + h0 * 640, 1280).rearrange(
                                "p (h d) -> p h d", d=640)[:, :, (nch_ - 1) * 128: nch_ * 128]
                            ii = v.tensor_tensor(
                                both_d, both_d,
                                b16(MKD, 128).unsqueeze(1).broadcast_to([128, 2, 128]),
                                ALU.mult)
                            if gt >= 4:
                                both_o = b16(APQ + h0 * 640, 1280).rearrange(
                                    "p (h d) -> p h d", d=640)[:, :, 0:128]
                                ii = v.tensor_tensor(
                                    both_o, both_o,
                                    b16(MKO, 128).unsqueeze(1).broadcast_to([128, 2, 128]),
                                    ALU.mult)
                            mask_g[g_] = inc(ii, "dve")
                        block.vector(_)

                for qv in range(4):
                    def _(t, qt=qt, qv=qv, gcs=gcs, nch_=nch_):
                        ws = [("dve", mask_g[qt * 8 + qv * 2 + 1])]
                        prev = (qt, qv - 1) if qv else ((qt - 1, 3) if qt else None)
                        if prev is not None:
                            ws.append(("dve", otvm[prev]))
                        waits(t, ws)
                        ii = None
                        for hh in range(4):
                            h = qv * 4 + hh
                            for ci in range(nch_):
                                ii = t.matmul(
                                    PS[:, 3072 + hh * 128: 3072 + (hh + 1) * 128],
                                    ones128, b16(APQ + h * 640 + ci * 128, 128),
                                    start=(ci == 0), stop=(ci == nch_ - 1),
                                    skip_group_check=True)
                        rs_mm[(qt, qv)] = inc(ii, "pe")
                        for hh in range(4):
                            h = qv * 4 + hh
                            kvh = h // 4
                            for ci, gc in enumerate(gcs):
                                pbc = (gc // NTIL) % 2
                                vts = b16(KV + pbc * 4096 + 2048 + (gc % NTIL) * 512
                                          + kvh * 128, 128)
                                ii = t.matmul(
                                    PS[:, 3584 + hh * 128: 3584 + (hh + 1) * 128],
                                    vts, b16(APQ + h * 640 + ci * 128, 128),
                                    start=(ci == 0), stop=(ci == nch_ - 1),
                                    skip_group_check=True)
                        otm[(qt, qv)] = inc(ii, "pe")
                    block.tensor(_)

                    def _(s, qt=qt, qv=qv):
                        waits(s, [("pe", rs_mm[(qt, qv)])])
                        s.activation(b32(LNT, 512), PS[:, 3072: 3072 + 512], AF.Ln)
                        ii = s.activation(b16(RCPW + qv * 512, 512), b32(LNT, 512),
                                          AF.Exp, scale=-1.0)
                        rcp_a[(qt, qv)] = inc(ii, "act")
                    block.scalar(_)

                    def _(v, qt=qt, qv=qv):
                        waits(v, [("pe", otm[(qt, qv)]), ("act", rcp_a[(qt, qv)])])
                        ii = v.tensor_tensor(
                            b16(Q_, 8192).rearrange("p (h t) -> p h t", t=512)
                            [:, qv * 4:(qv + 1) * 4, qt * 128:(qt + 1) * 128],
                            PS[:, 3584: 3584 + 512].rearrange("p (h t) -> p h t", t=128),
                            b16(RCPW + qv * 512, 512).rearrange("p (h t) -> p h t", t=128),
                            ALU.mult)
                        otvm[(qt, qv)] = inc(ii, "dve")
                    block.vector(_)

            rope_last = qn_dve[15]

            if "ot" in tap_d:
                def _(sy):
                    waits(sy, [("dve", otvm[(NTIL - 1, 3)])])
                    ii = sy.dma_start(out=tap_d["ot"][scn], in_=b16(Q_, 8192))
                    inc(ii, "st", 16)
                block.sync(_)

            nc.leave_named_scope("sc%d_C1" % scn, _scope_id, False)
            _scope_id, _ = nc.enter_named_scope("sc%d_D" % scn, False)
            s_mm = {}
            s_ev = {}
            for qr in range(4):
                for tt in range(NTIL):
                    un = qr * NTIL + tt

                    def _(t, qr=qr, tt=tt, un=un):
                        ws = [slabm["wo%d" % qr],
                              ("dve", otvm[(tt, 3)])]
                        if un >= 4:
                            ws.append(("dve", s_ev[un - 4]))
                        else:
                            ws.append(("act", exp_g[ngroups - 1]))
                        waits(t, ws)
                        buf = R1U if qr % 2 == 0 else R1L
                        sb = PS[:, (un % 4) * 512:((un % 4) + 1) * 512]
                        ii = None
                        for c in range(NCH):
                            ii = t.matmul(sb, b16(Q_ + c * 512 + tt * 128, 128),
                                          b16(buf + c * 512, 512),
                                          start=(c == 0), stop=(c == NCH - 1))
                        s_mm[un] = inc(ii, "pe")
                    block.tensor(_)

                    def _(v, qr=qr, tt=tt, un=un):
                        waits(v, [("pe", s_mm[un])])
                        sb = PS[:, (un % 4) * 512:((un % 4) + 1) * 512]
                        ii = v.tensor_copy(b16(P_ + tt * 2048 + qr * 512, 512), sb)
                        s_ev[un] = inc(ii, "dve")
                    block.vector(_)

                if qr == 0:
                    emit_wo_load(2, [("pe", s_mm[NTIL - 1])])
                elif qr == 1:
                    emit_wo_load(3, [("pe", s_mm[NTIL + NTIL - 1])])

            if "ssb" in tap_d:
                def _(sy):
                    waits(sy, [("dve", s_ev[15])])
                    ii = sy.dma_start(out=tap_d["ssb"][scn], in_=b16(P_, 8192))
                    inc(ii, "st", 16)
                block.sync(_)

            nc.leave_named_scope("sc%d_D" % scn, _scope_id, False)
            _scope_id, _ = nc.enter_named_scope("sc%d_E1" % scn, False)
            gu_mm = {}
            sil_m = {}
            fu_dve = {}
            wgm = {}
            wum = {}
            wo2_done = s_mm[2 * NTIL + NTIL - 1]

            def emit_gu_load(ib):
                def _(g, ib=ib):
                    if ib < 2:
                        waits(g, [("pe", wo2_done)])
                    else:
                        waits(g, [("pe", gu_mm[ib - 2])])
                    ii = g.dma_start(out=b16(WGB[ib % 2], 2048), in_=wg_d[ib])
                    wgm[ib] = ("wgs%d" % (ib % 2), inc(ii, "wgs%d" % (ib % 2), 16))
                    ii = g.dma_start(out=b16(WUB[ib % 2], 2048), in_=wu_d[ib])
                    wum[ib] = ("wus%d" % (ib % 2), inc(ii, "wus%d" % (ib % 2), 16))
                block.gpsimd(_)

            emit_gu_load(0)
            emit_gu_load(1)

            for ib in range(NIB):
                def _(t, ib=ib):
                    ws = [wgm[ib], wum[ib]]
                    if ib >= 2:
                        ws.append(("act", sil_m[ib - 2]))
                        ws.append(("dve", fu_dve[ib - 2]))
                    else:
                        ws.append(("dve", s_ev[15]))
                    waits(t, ws)
                    gp = PS[:, (ib % 2) * 1024:(ib % 2) * 1024 + 512]
                    up = PS[:, (ib % 2) * 1024 + 512:(ib % 2) * 1024 + 1024]
                    ii = None
                    for c in range(NCH):
                        t.matmul(gp, b16(WGB[ib % 2] + c * 128, 128),
                                 b16(XNT + c * 512, 512),
                                 start=(c == 0), stop=(c == NCH - 1))
                    for c in range(NCH):
                        ii = t.matmul(up, b16(WUB[ib % 2] + c * 128, 128),
                                      b16(XNT + c * 512, 512),
                                      start=(c == 0), stop=(c == NCH - 1))
                    gu_mm[ib] = inc(ii, "pe")
                block.tensor(_)

                if ib + 2 < NIB:
                    emit_gu_load(ib + 2)

                def _(s, ib=ib):
                    waits(s, [("pe", gu_mm[ib]), ("wg", consts)])
                    gp = PS[:, (ib % 2) * 1024:(ib % 2) * 1024 + 512]
                    ii = s.activation(b16(SILT[ib % 2], 512), gp, AF.Silu,
                                      scale=b32(SGU + ib, 1))
                    sil_m[ib] = inc(ii, "act")
                block.scalar(_)

                def _(v, ib=ib):
                    waits(v, [("act", sil_m[ib])])
                    up = PS[:, (ib % 2) * 1024 + 512:(ib % 2) * 1024 + 1024]
                    v.tensor_scalar_mul(b16(UPT[ib % 2], 512), up,
                                        b32(SGU + 64 + ib, 1))
                    ii = v.tensor_tensor(b16(FU + ib * 512, 512),
                                         b16(SILT[ib % 2], 512),
                                         b16(UPT[ib % 2], 512), ALU.mult)
                    fu_dve[ib] = inc(ii, "dve")
                block.vector(_)

            if "fuse" in tap_d:
                def _(sy):
                    waits(sy, [("dve", fu_dve[NIB - 1])])
                    ii = sy.dma_start(out=tap_d["fuse"][scn], in_=b16(FU, 32768))
                    inc(ii, "st", 16)
                block.sync(_)

            nc.leave_named_scope("sc%d_E1" % scn, _scope_id, False)
            if scn + 1 < NSEQ * NSCQ:
                emit_apre(scn + 1,
                          gates=[("pe", s_mm[3 * NTIL + NTIL - 1]),
                                 ("dve", otvm[(NTIL - 1, 3)]),
                                 ("pe", a_tr[NTIL - 1])])
            _scope_id, _ = nc.enter_named_scope("sc%d_E2" % scn, False)
            dn_mm = {}
            wdm = {}
            gsc_m = {}
            f_dve = {}
            f_act = {}
            xf_mark = {}

            NWG = NIB // 2
            for hf in range(2):
                def emit_wd_load(gw, hf=hf):
                    def _(g, gw=gw, hf=hf):
                        gg = hf * NWG + gw
                        slot = gg % 4
                        if gg < 2:
                            waits(g, [("pe", s_mm[3 * NTIL + NTIL - 1])])
                        elif gg < 4:
                            waits(g, [("pe", gu_mm[NIB - 1])])
                        else:
                            pg = gg - 4
                            phf, pgw = divmod(pg, NWG)
                            waits(g, [("pe", dn_mm[(phf, 2 * pgw + 1)])])
                        ii = g.dma_start(
                            out=b16(WDB[slot], 2048).rearrange("p (g n) -> p g n", n=1024),
                            in_=wd_d[2 * gw:2 * gw + 2, :,
                                     hf * 1024:(hf + 1) * 1024].rearrange("g p n -> p g n"))
                        wdm[(hf, gw)] = ("wds%d" % slot, inc(ii, "wds%d" % slot, 16))
                    block.gpsimd(_)

                if hf == 0:
                    pend_wd = []
                    for _g in range(4):
                        emit_wd_load(_g)
                else:
                    for _g in pend_wd:
                        emit_wd_load(_g)
                    pend_wd = []

                for ts in range(2):
                    def _(sy, ts=ts, hf=hf):
                        waits(sy, [("dve", last_xf_dve[ts % 2])])
                        ii = sy.dma_start(out=b32(XF[ts % 2], 1024),
                                     in_=x_d[si, base + ts * 128: base + (ts + 1) * 128,
                                             hf * 1024:(hf + 1) * 1024])
                        xf_mark[(hf, ts)] = ("xf%d" % (ts % 2), inc(ii, "xf%d" % (ts % 2), 16))
                    block.sync(_)

                for gw in range(NWG):
                    def _(t, gw=gw, hf=hf):
                        ws = [wdm[(hf, gw)]]
                        if gw == 0:
                            if hf == 0:
                                ws.append(("dve", fu_dve[NIB - 1]))
                                ws.append(("act", sil_m[NIB - 1]))
                            else:
                                for ts2 in range(NTIL):
                                    ws.append(("dve", gsc_m[(0, ts2)]))
                        waits(t, ws)
                        ii = None
                        for ibh in range(2):
                            ib = 2 * gw + ibh
                            slot = (hf * NWG + gw) % 4
                            for ts in range(NTIL):
                                lh = b16(FU + ib * 512 + ts * 128, 128)
                                ii = t.matmul(PS[:, ts * 1024: ts * 1024 + 512], lh,
                                              b16(WDB[slot] + ibh * 1024, 512),
                                              start=(ib == 0), stop=(ib == NIB - 1))
                                ii = t.matmul(PS[:, ts * 1024 + 512: (ts + 1) * 1024], lh,
                                              b16(WDB[slot] + ibh * 1024 + 512, 512),
                                              start=(ib == 0), stop=(ib == NIB - 1))
                            dn_mm[(hf, ib)] = inc(ii, "pe")
                    block.tensor(_)

                    gg = hf * NWG + gw
                    ng = gg + 4
                    if ng < 2 * NWG:
                        nhf, ngw = divmod(ng, NWG)
                        if nhf == hf:
                            emit_wd_load(ngw)
                        else:
                            pend_wd.append(ngw)

                def _(v, hf=hf):
                    waits(v, [("pe", dn_mm[(hf, NIB - 1)]), ("wg", consts)])
                    for ts in range(NTIL):
                        ii = v.tensor_tensor(gsc_buf(ts),
                                             PS[:, ts * 1024:(ts + 1) * 1024],
                                             b32(SDN + hf * 1024, 1024), ALU.mult)
                        gsc_m[(hf, ts)] = inc(ii, "dve")
                block.vector(_)

                for ts in range(NTIL):
                    tapm = None
                    if "gsc" in tap_d:
                        def _(sy, ts=ts, hf=hf):
                            waits(sy, [("dve", gsc_m[(hf, ts)])])
                            ii = sy.dma_start(out=tap_d["gsc"][scn, hf * 4 + ts],
                                              in_=gsc_buf(ts))
                            inc(ii, "st", 16)
                        block.sync(_)
                        tapm = cnt["st"]

                    def _(s, ts=ts, hf=hf):
                        waits(s, [("dve", gsc_m[(hf, ts)])])
                        ii = s.activation(b16(GSIL[ts % 2], 1024), gsc_buf(ts),
                                          AF.Silu)
                        f_act[(hf, ts)] = inc(ii, "act")
                    block.scalar(_)

                    def _(v, ts=ts, hf=hf, tapm=tapm):
                        waits(v, [("act", f_act[(hf, ts)]), xf_mark[(hf, ts)],
                                  last_out_st[ts % 2]])
                        if tapm:
                            waits(v, [("st", tapm)])
                        v.tensor_tensor(gsc_buf(ts), b16(GSIL[ts % 2], 1024),
                                        b16(P_ + ts * 2048 + hf * 1024, 1024), ALU.mult)
                        ii = v.tensor_tensor(b32(OUTF[ts % 2], 1024), gsc_buf(ts),
                                             b32(XF[ts % 2], 1024), ALU.add)
                        f_dve[(hf, ts)] = inc(ii, "dve")
                    block.vector(_)
                    last_xf_dve[ts % 2] = f_dve[(hf, ts)]

                    def _(sy, ts=ts, hf=hf):
                        waits(sy, [("dve", f_dve[(hf, ts)])])
                        ii = sy.dma_start(out=out_d[si, base + ts * 128: base + (ts + 1) * 128,
                                               hf * 1024:(hf + 1) * 1024],
                                     in_=b32(OUTF[ts % 2], 1024))
                        last_out_st[ts % 2] = ("st%d" % (ts % 2), inc(ii, "st%d" % (ts % 2), 16))
                    block.sync(_)

                    if ts + 2 < NTIL:
                        def _(sy, ts=ts, hf=hf):
                            ts2 = ts + 2
                            waits(sy, [("dve", last_xf_dve[ts2 % 2])])
                            ii = sy.dma_start(out=b32(XF[ts2 % 2], 1024),
                                         in_=x_d[si, base + ts2 * 128:
                                                 base + (ts2 + 1) * 128,
                                                 hf * 1024:(hf + 1) * 1024])
                            xf_mark[(hf, ts2)] = ("xf%d" % (ts2 % 2), inc(ii, "xf%d" % (ts2 % 2), 16))
                        block.sync(_)

            nc.leave_named_scope("sc%d_E2" % scn, _scope_id, False)
            if sc_barrier:
                def _(t):
                    waits(t, [("dve", cnt["dve"]), ("act", cnt["act"]),
                              ("st", cnt["st"]), ("wg", cnt["wg"]), ("xl", cnt["xl"])])
                    ii = t.nop()
                    inc(ii, "pe")
                block.tensor(_)
                for engf in (block.vector, block.scalar, block.gpsimd, block.sync):
                    def _(e):
                        waits(e, [("pe", cnt["pe"])])
                    engf(_)
            bnd["pe_E2"] = dn_mm[(1, NIB - 1)]
            bnd["pe_D3"] = s_mm[3 * NTIL + NTIL - 1]
            bnd["otv_last"] = otvm[(NTIL - 1, 3)]
            bnd["fu_dve"] = fu_dve[NIB - 1]
            bnd["dve_F"] = f_dve[(1, NTIL - 1)]
            bnd["act_F"] = f_act[(1, NTIL - 1)]
            bnd["bank03_dve"] = gsc_m[(1, 1)]
            bnd["bank47_dve"] = gsc_m[(1, 3)]
            bnd["rope_dve"] = rope_last
            bnd["a_tr_last"] = a_tr[NTIL - 1]

        def _(sy):
            if cnt["st"] > 0:
                sy.wait_ge(hST, cnt["st"])
            sy.wait_ge(sems["st0"], cnt["st0"])
            sy.wait_ge(sems["st1"], cnt["st1"])
        block.sync(_)

    return nc


def _host_pack(inputs):
    wq = np.ascontiguousarray(inputs["L0_attn_wq"])
    wk = np.ascontiguousarray(inputs["L0_attn_wk"])
    wv = np.ascontiguousarray(inputs["L0_attn_wv"])
    wo = np.ascontiguousarray(inputs["L0_attn_wo"])
    wg = np.ascontiguousarray(
        inputs["L0_ffn_w_gate"].reshape(NCH, 128, NIB, 128).transpose(2, 1, 0, 3)
        .reshape(NIB, 128, HID))
    wu = np.ascontiguousarray(
        inputs["L0_ffn_w_up"].reshape(NCH, 128, NIB, 128).transpose(2, 1, 0, 3)
        .reshape(NIB, 128, HID))
    wd = np.ascontiguousarray(inputs["L0_ffn_w_down"].reshape(NIB, 128, HID))
    sg = np.ascontiguousarray(inputs["L0_ffn_s_gate"].reshape(NIB, 128).T)
    su = np.ascontiguousarray(inputs["L0_ffn_s_up"].reshape(NIB, 128).T)
    sd = np.ascontiguousarray(inputs["L0_ffn_s_down"].reshape(1, HID))

    idm = np.eye(128, dtype=BF)
    ones = np.ones((128, 128), dtype=BF)
    pp = np.arange(128)
    mkd = (pp[:, None] <= pp[None, :]).astype(BF)
    mko = (pp[:, None] > pp[None, :]).astype(BF)

    inv_freq = 1.0 / 10000.0 ** (np.arange(0, DH, 2, dtype=np.float64) / DH)
    pos = np.arange(T, dtype=np.float64)
    fr = pos[:, None] * inv_freq[None, :]
    cos = np.concatenate([np.cos(fr), np.cos(fr)], -1).astype(np.float32)
    sin = np.concatenate([np.sin(fr), np.sin(fr)], -1).astype(np.float32)
    snneg = np.concatenate([-sin[:, :DH // 2], sin[:, DH // 2:]], -1)
    cosp = np.ascontiguousarray(
        cos.reshape(T // 128, 128, DH).transpose(1, 0, 2).reshape(128, -1)).astype(BF)
    snp = np.ascontiguousarray(
        snneg.reshape(T // 128, 128, DH).transpose(1, 0, 2).reshape(128, -1)).astype(BF)

    eps = np.array([[1e-5, 1e-18]], dtype=np.float32)
    return dict(wq=wq, wk=wk, wv=wv, wo=wo, wg=wg, wu=wu, wd=wd, sg=sg, su=su,
                sd=sd, eps=eps, idm=idm, ones=ones, mkd=mkd, mko=mko, cosp=cosp,
                snp=snp)


def kernel(**inputs):
    x = np.ascontiguousarray(inputs["L0_attn_x"])
    shared = _host_pack(inputs)
    nc = build_program()
    in_maps = []
    for c in range(NCORE):
        m = dict(shared)
        m["x"] = np.ascontiguousarray(x[c * NSEQ:(c + 1) * NSEQ])
        in_maps.append(m)
    res = run_bass_kernel_spmd(nc, in_maps, core_ids=list(range(NCORE)))
    out = np.concatenate([r["out"] for r in res.results], axis=0)
    return np.ascontiguousarray(out.astype(np.float32))


# version 17
# speedup vs baseline: 1.0131x; 1.0131x over previous
import numpy as np
import ml_dtypes
import concourse.bass as bass
import concourse.mybir as mybir
from concourse.bass_utils import run_bass_kernel_spmd
from concourse.alu_op_type import AluOpType as ALU

F32 = mybir.dt.float32
BF16 = mybir.dt.bfloat16
I8 = mybir.dt.int8
AF = mybir.ActivationFunctionType
AX = mybir.AxisListType
BF = ml_dtypes.bfloat16

B, T, HID = 16, 2048, 2048
NH, NKV, DH = 16, 4, 128
INTER = 8192
NCORE = 8
NSEQ = 2
SC = 512
NTIL = 4
NSCQ = T // SC
NCH = HID // 128
NIB = INTER // 128
SCALE = 1.0 / np.sqrt(DH)

XNT = 0
R1U = 8192
R1L = 16384
KV = 24576
P_ = 32768
Q_ = 40960
FU = 49152
IDM = 81920
ONESC = 82048
MKD = 82176
MKO = 82304
COSB = 82432
SNB = 82944
A16N = 83456

XN0, XN1 = Q_ + 4096, Q_ + 6144
QSB = FU + 4096
QSQ = FU + 6144
QS_ = FU + 8192
RA_ = FU + 10240
RB_ = FU + 12288
QN_ = FU + 14336
AEX = (FU + 16384, FU + 17664)
RCP = (FU + 18944, FU + 19200)

WGB = (R1U + 0, R1U + 2048)
WUB = (R1U + 4096, R1U + 6144)
WDB = (R1L + 0, R1L + 2048, R1U + 4096, R1U + 6144)
SILT = (R1L + 4096, R1L + 4608)
UPT = (R1L + 5120, R1L + 5632)
GSIL = (R1L + 6144, R1L + 7168)

XA = (0, 2048)
GSC = 0
STAT = 4096
QSS = 4224
EPSC = 4272
LNR = 4288
SDN = 4544
SGU = 6592
XF = (6720, 7744)
OUTF = (8768, 9792)
A32N = 10816


def build_program(taps=(), sc_barrier=False, e2_serial=False):
    nc = bass.Bass("TRN2", target_bir_lowering=False)
    x_d = nc.dram_tensor("x", [NSEQ, T, HID], F32, kind="ExternalInput").ap()
    wq_d = nc.dram_tensor("wq", [HID, NH * DH], F32, kind="ExternalInput").ap()
    wk_d = nc.dram_tensor("wk", [HID, NKV * DH], F32, kind="ExternalInput").ap()
    wv_d = nc.dram_tensor("wv", [HID, NKV * DH], F32, kind="ExternalInput").ap()
    wo_d = nc.dram_tensor("wo", [NH * DH, HID], F32, kind="ExternalInput").ap()
    wg_d = nc.dram_tensor("wg", [NIB, 128, HID], I8, kind="ExternalInput").ap()
    wu_d = nc.dram_tensor("wu", [NIB, 128, HID], I8, kind="ExternalInput").ap()
    wd_d = nc.dram_tensor("wd", [NIB, 128, HID], I8, kind="ExternalInput").ap()
    sg_d = nc.dram_tensor("sg", [128, NIB], F32, kind="ExternalInput").ap()
    su_d = nc.dram_tensor("su", [128, NIB], F32, kind="ExternalInput").ap()
    sd_d = nc.dram_tensor("sd", [1, HID], F32, kind="ExternalInput").ap()
    eps_d = nc.dram_tensor("eps", [1, 2], F32, kind="ExternalInput").ap()
    idm_d = nc.dram_tensor("idm", [128, 128], BF16, kind="ExternalInput").ap()
    on_d = nc.dram_tensor("ones", [128, 128], BF16, kind="ExternalInput").ap()
    mkd_d = nc.dram_tensor("mkd", [128, 128], BF16, kind="ExternalInput").ap()
    mko_d = nc.dram_tensor("mko", [128, 128], BF16, kind="ExternalInput").ap()
    cos_d = nc.dram_tensor("cosp", [128, T], BF16, kind="ExternalInput").ap()
    sn_d = nc.dram_tensor("snp", [128, T], BF16, kind="ExternalInput").ap()
    out_d = nc.dram_tensor("out", [NSEQ, T, HID], F32, kind="ExternalOutput").ap()
    wqb_d = nc.dram_tensor("wqb", [HID, NH * DH], BF16, kind="Internal").ap()
    wkb_d = nc.dram_tensor("wkb", [HID, NKV * DH], BF16, kind="Internal").ap()
    wvb_d = nc.dram_tensor("wvb", [HID, NKV * DH], BF16, kind="Internal").ap()
    wob_d = nc.dram_tensor("wob", [NH * DH, HID], BF16, kind="Internal").ap()

    tap_d = {}
    for tp, size in (("xnt", 8192), ("kv", 8192), ("qt", 8192), ("ot", 8192),
                     ("ssb", 8192), ("fuse", 32768)):
        if tp in taps:
            tap_d[tp] = nc.dram_tensor(
                "tap_" + tp, [NSEQ * NSCQ, 128, size], BF16, kind="ExternalOutput").ap()
    if "gsc" in taps:
        tap_d["gsc"] = nc.dram_tensor(
            "tap_gsc", [NSEQ * NSCQ, 8, 128, 1024], F32, kind="ExternalOutput").ap()

    import contextlib
    _es = contextlib.ExitStack()
    _slotnames = ["slU", "slL", "cos", "wgs0", "wgs1", "wus0", "wus1",
                  "wds0", "wds1", "wds2", "wds3", "xa0", "xa1", "xf0", "xf1", "st0", "st1"]
    with (
        nc.sbuf_tensor("a16", [128, A16N], BF16) as a16_t,
        nc.sbuf_tensor("a32", [128, A32N], F32) as a32_t,
        nc.psum_tensor("ps", [128, 4096], F32) as ps_t,
        nc.semaphore("sWG") as hWG,
        nc.semaphore("sXL") as hXL,
        nc.semaphore("sST") as hST,
        nc.semaphore("sPE") as hPE,
        nc.semaphore("sACT") as hACT,
        nc.semaphore("sDVE") as hDVE,
        nc.Block() as block,
        _es,
    ):
        _slot_h = {n: _es.enter_context(nc.semaphore("s_" + n)) for n in _slotnames}
        PS = ps_t[:, :]
        PSB = PS.bitcast(BF16)
        def gsc_buf(ts):
            return b16(Q_ + ts * 1024, 1024)

        sems = {"wg": hWG, "xl": hXL, "st": hST, "pe": hPE, "act": hACT, "dve": hDVE}
        sems.update(_slot_h)
        cnt = {k: 0 for k in sems}

        def inc(instr, name, step=1):
            cnt[name] += step
            instr.then_inc(sems[name], step)
            return cnt[name]

        def waits(eng, ws):
            for name, tgt in ws:
                if tgt > 0:
                    eng.wait_ge(sems[name], tgt)

        def b16(off, n):
            return a16_t[:, off:off + n]

        def b32(off, n):
            return a32_t[:, off:off + n]

        ident = b16(IDM, 128)
        ones128 = b16(ONESC, 128)

        def _(g):
            ii = g.dma_start(out=b16(IDM, 128), in_=idm_d).then_inc(hWG, 16)
            ii = g.dma_start(out=b16(ONESC, 128), in_=on_d).then_inc(hWG, 16)
            ii = g.dma_start(out=b16(MKD, 128), in_=mkd_d).then_inc(hWG, 16)
            ii = g.dma_start(out=b16(MKO, 128), in_=mko_d).then_inc(hWG, 16)
            ii = g.dma_start(out=b32(SDN, 2048),
                        in_=sd_d[0:1, :].broadcast_to([128, HID])).then_inc(hWG, 16)
            ii = g.dma_start(out=b32(SGU, 64), in_=sg_d).then_inc(hWG, 16)
            ii = g.dma_start(out=b32(SGU + 64, 64), in_=su_d).then_inc(hWG, 16)
            ii = g.dma_start(out=b32(EPSC, 2),
                             in_=eps_d[0:1, :].broadcast_to([128, 2])).then_inc(hWG, 16)
        block.gpsimd(_)
        cnt["wg"] += 8 * 16
        consts = cnt["wg"]

        wcv = [(wq_d, wqb_d, NH * DH), (wk_d, wkb_d, NKV * DH),
               (wv_d, wvb_d, NKV * DH), (wo_d, wob_d, HID)]
        pro_st = None
        for wi, (srcw, dstw, ow) in enumerate(wcv):
            def _(g, wi=wi, srcw=srcw, ow=ow):
                if wi >= 1:
                    waits(g, [("st", cnt["st"])])
                ii = g.dma_start(
                    out=b16(FU, 16 * ow).rearrange("p (c n) -> p c n", n=ow),
                    in_=srcw.rearrange("(c p) n -> p c n", p=128))
                inc(ii, "wg", 16)
            block.gpsimd(_)
            def _(sy, wi=wi, dstw=dstw, ow=ow):
                waits(sy, [("wg", cnt["wg"])])
                ii = sy.dma_start(
                    out=dstw.rearrange("(c p) n -> p c n", p=128),
                    in_=b16(FU, 16 * ow).rearrange("p (c n) -> p c n", n=ow))
                inc(ii, "st", 16)
            block.sync(_)
        pro_st = cnt["st"]

        bnd = {"pe_E2": 0, "fu_dve": 0, "dve_F": 0, "act_F": 0,
               "rope_dve": 0, "pe_D3": 0, "otv_last": 0, "a_tr_last": 0,
               "bank03_dve": 0, "bank47_dve": 0}
        last_xf_dve = {0: 0, 1: 0}
        last_out_st = {0: ("st0", 0), 1: ("st1", 0)}

        apre = {}

        def emit_apre(scn2, gates=()):
            si2, u2 = divmod(scn2, NSCQ)
            base2 = u2 * SC
            marks = {"xn": [0] * NTIL, "xl": [None] * NTIL, "stats": [0] * NTIL}
            apre[scn2] = marks
            prev = apre.get(scn2 - 1)
            for tt in range(NTIL):
                xa_off = XA[tt % 2]
                xn_off = (XN0, XN1)[tt % 2]
                st = STAT + tt * 32

                def _(sy, tt=tt, xa_off=xa_off):
                    ws = []
                    if prev is not None:
                        ws.append(("act", prev["xn"][(tt % 2) + 2]))
                    if tt >= 2:
                        ws.append(("act", marks["xn"][tt - 2]))
                    waits(sy, ws)
                    ii = sy.dma_start(out=b32(xa_off, 2048),
                                 in_=x_d[si2, base2 + tt * 128: base2 + (tt + 1) * 128, :])
                    marks["xl"][tt] = ("xa%d" % (tt % 2), inc(ii, "xa%d" % (tt % 2), 16))
                block.sync(_)

                def _(v, tt=tt, xa_off=xa_off, st=st):
                    waits(v, [marks["xl"][tt]])
                    ii = None
                    for i in range(4):
                        ii = v.bn_stats(b32(st, 24)[:, i * 6:(i + 1) * 6],
                                        b32(xa_off, 2048)[:, i * 512:(i + 1) * 512])
                    inc(ii, "dve")
                    v.wait_ge(hDVE, cnt["dve"])
                    ii = v.bn_aggr(b32(st + 24, 2), b32(st, 24))
                    inc(ii, "dve")
                block.vector(_)

                def _(s, tt=tt, st=st):
                    waits(s, [("dve", cnt["dve"]), ("wg", consts)])
                    s.activation(b32(st + 26, 1), b32(st + 24, 2)[:, 1:2], AF.Ln,
                                 bias=b32(EPSC, 1))
                    ii = s.activation(b32(st + 27, 1), b32(st + 26, 1), AF.Exp,
                                      scale=-0.5)
                    inc(ii, "act")
                block.scalar(_)

                def _(v, tt=tt, st=st):
                    waits(v, [("act", cnt["act"])])
                    ii = v.scalar_tensor_tensor(b32(st + 28, 1),
                                                b32(st + 24, 2)[:, 0:1], -1.0,
                                                b32(st + 27, 1), ALU.mult, ALU.mult)
                    marks["stats"][tt] = inc(ii, "dve")
                block.vector(_)

                if tt < 2:
                    def _(s, tt=tt, xa_off=xa_off, xn_off=xn_off, st=st,
                          gates=tuple(gates)):
                        ws = [("dve", marks["stats"][tt])]
                        ws.extend(gates)
                        waits(s, ws)
                        ii = s.activation(b16(xn_off, 2048), b32(xa_off, 2048),
                                          AF.Identity,
                                          scale=b32(st + 27, 1), bias=b32(st + 28, 1))
                        marks["xn"][tt] = inc(ii, "act")
                    block.scalar(_)

        emit_apre(0)

        for scn in range(NSEQ * NSCQ):
            si, u = divmod(scn, NSCQ)
            pb = u % 2
            base = u * SC
            slabm = {}
            amk = apre[scn]
            xn_act = amk["xn"]

            def _(sq):
                if scn == 0:
                    waits(sq, [("st", pro_st)])
                waits(sq, [("pe", bnd["pe_E2"]), ("dve", bnd["fu_dve"])])
                ii = sq.dma_start(out=b16(R1U, 8192).rearrange("p (c n) -> p c n", n=512),
                            in_=wkb_d.rearrange("(c p) n -> p c n", p=128))
                slabm["wk"] = ("slU", inc(ii, "slU", 16))
                waits(sq, [("dve", bnd["dve_F"]), ("pe", bnd["pe_E2"])])
                ii = sq.dma_start(out=b16(R1L, 8192).rearrange("p (c n) -> p c n", n=512),
                            in_=wvb_d.rearrange("(c p) n -> p c n", p=128))
                slabm["wv"] = ("slL", inc(ii, "slL", 16))
            block.sync(_)

            def _(g):
                waits(g, [("dve", bnd["rope_dve"])])
                ii = g.dma_start(out=b16(COSB, 512), in_=cos_d[:, u * 512:(u + 1) * 512])
                inc(ii, "cos", 16)
                ii = g.dma_start(out=b16(SNB, 512), in_=sn_d[:, u * 512:(u + 1) * 512])
                slabm["cos"] = ("cos", inc(ii, "cos", 16))
            block.gpsimd(_)

            _scope_id, _ = nc.enter_named_scope("sc%d_A" % scn, False)
            a_tr = [0] * NTIL
            a_evac = [0] * NTIL
            xn_act = [0] * NTIL
            xl_mark = [0] * NTIL
            k_mm = [0] * NTIL
            v_mm = [0] * NTIL
            kt_tr = [0] * NTIL
            kT_ev = [0] * NTIL
            v_ev = [0] * NTIL
            kn_dve = [0] * NTIL
            k_ev_act = [0] * NTIL
            kbank = [4, 6, 4, 6]
            vbank = [5, 7, 5, 7]

            def emit_ktr(tt):
                def _(t, tt=tt):
                    ws = [("dve", kn_dve[tt])]
                    if tt >= 2:
                        ws.append(("dve", kT_ev[tt - 2]))
                    waits(t, ws)
                    slot = (tt % 2) * 2048
                    ii = None
                    for h in range(NKV):
                        ii = t.transpose(PSB[:, slot + h * 128: slot + (h + 1) * 128],
                                         b16(QN_ + (tt % 2) * 512 + h * 128, 128), ident)
                    kt_tr[tt] = inc(ii, "pe")
                block.tensor(_)

                def _(v, tt=tt):
                    waits(v, [("pe", kt_tr[tt])])
                    slot = (tt % 2) * 2048
                    ii = v.tensor_copy(
                        b16(KV + pb * 4096, 2048).rearrange("p (h t) -> p h t", t=512)
                        [:, :, tt * 128:(tt + 1) * 128],
                        PSB[:, slot:slot + 512].rearrange("p (h t) -> p h t", t=128))
                    kT_ev[tt] = inc(ii, "dve")
                block.vector(_)

            for tt in range(NTIL):
                xn_off = (XN0, XN1, Q_ + 0, Q_ + 2048)[tt]
                if tt >= 2:
                    def _(s, tt=tt, xn_off=xn_off):
                        st = STAT + tt * 32
                        waits(s, [("dve", amk["stats"][tt]), ("dve", bnd["dve_F"])])
                        ii = s.activation(b16(xn_off, 2048), b32(XA[tt % 2], 2048),
                                          AF.Identity,
                                          scale=b32(st + 27, 1), bias=b32(st + 28, 1))
                        amk["xn"][tt] = inc(ii, "act")
                    block.scalar(_)

                def _(t, tt=tt, xn_off=xn_off):
                    ws = [("act", amk["xn"][tt]), ("wg", consts)]
                    if tt >= 2:
                        ws.append(("dve", a_evac[tt - 2]))
                        ws.append(("dve", kT_ev[tt - 2]))
                    else:
                        ws.append(("dve", bnd["bank03_dve"]))
                    waits(t, ws)
                    slot = (tt % 2) * 2048
                    ii = None
                    for c in range(NCH):
                        ii = t.transpose(PSB[:, slot + c * 128: slot + (c + 1) * 128],
                                         b16(xn_off, 2048)[:, c * 128:(c + 1) * 128],
                                         ident)
                    a_tr[tt] = inc(ii, "pe")
                block.tensor(_)

                def _(v, tt=tt):
                    waits(v, [("pe", a_tr[tt])])
                    slot = (tt % 2) * 2048
                    ii = v.tensor_copy(
                        b16(XNT, 8192).rearrange("p (c t) -> p c t", t=512)
                        [:, :, tt * 128:(tt + 1) * 128],
                        PSB[:, slot:slot + 2048].rearrange("p (c t) -> p c t", t=128))
                    a_evac[tt] = inc(ii, "dve")
                block.vector(_)

                def _(t, tt=tt):
                    ws = [("dve", a_evac[tt]), slabm["wk"]]
                    if tt >= 2:
                        ws.append(("act", k_ev_act[tt - 2]))
                    else:
                        ws.append(("dve", bnd["bank47_dve"]))
                    waits(t, ws)
                    kb = PS[:, kbank[tt] * 512:(kbank[tt] + 1) * 512]
                    ii = None
                    for c in range(NCH):
                        ii = t.matmul(kb, b16(XNT + c * 512 + tt * 128, 128),
                                      b16(R1U + c * 512, 512),
                                      start=(c == 0), stop=(c == NCH - 1))
                    k_mm[tt] = inc(ii, "pe")
                block.tensor(_)

                def _(s, tt=tt):
                    ws = [("pe", k_mm[tt])]
                    if tt >= 2:
                        ws.append(("dve", kn_dve[tt - 2]))
                    waits(s, ws)
                    kb = PS[:, kbank[tt] * 512:(kbank[tt] + 1) * 512]
                    ii = s.activation(b16(QSB + (tt % 2) * 512, 512), kb, AF.Copy)
                    k_ev_act[tt] = inc(ii, "act")
                block.scalar(_)

                def _(v, tt=tt):
                    waits(v, [("act", k_ev_act[tt])])
                    v.tensor_tensor(b16(QSQ, 512), b16(QSB + (tt % 2) * 512, 512),
                                    b16(QSB + (tt % 2) * 512, 512), ALU.mult)
                    ii = v.tensor_reduce(
                        b32(QSS, 4),
                        b16(QSQ, 512).rearrange("p (h d) -> p h d", d=128),
                        AX.X, ALU.add)
                    inc(ii, "dve")
                block.vector(_)

                def _(s, tt=tt):
                    waits(s, [("dve", cnt["dve"]), ("wg", consts)])
                    s.activation(b32(QSS + 8, 4), b32(QSS, 4), AF.Ln, bias=b32(EPSC + 1, 1))
                    ii = s.activation(b32(QSS + 12, 4), b32(QSS + 8, 4), AF.Exp,
                                      scale=-0.5)
                    inc(ii, "act")
                block.scalar(_)

                def _(v, tt=tt):
                    ws = [("act", cnt["act"]), slabm["cos"]]
                    if tt >= 2:
                        ws.append(("pe", kt_tr[tt - 2]))
                    waits(v, ws)
                    ks3 = b16(QS_, 512).rearrange("p (h d) -> p h d", d=128)
                    v.tensor_tensor(ks3,
                                    b16(QSB + (tt % 2) * 512, 512).rearrange("p (h d) -> p h d", d=128),
                                    b32(QSS + 12, 4).unsqueeze(2).broadcast_to([128, 4, 128]),
                                    ALU.mult)
                    cosr = b16(COSB + tt * 128, 128).unsqueeze(1).broadcast_to([128, 4, 128])
                    v.tensor_tensor(b16(RA_, 512).rearrange("p (h d) -> p h d", d=128),
                                    ks3, cosr, ALU.mult)
                    sn1 = b16(SNB + tt * 128, 64).unsqueeze(1).broadcast_to([128, 4, 64])
                    sn2 = b16(SNB + tt * 128 + 64, 64).unsqueeze(1).broadcast_to([128, 4, 64])
                    rb3 = b16(RB_, 512).rearrange("p (h d) -> p h d", d=128)
                    v.tensor_tensor(rb3[:, :, 0:64], ks3[:, :, 64:128], sn1, ALU.mult)
                    v.tensor_tensor(rb3[:, :, 64:128], ks3[:, :, 0:64], sn2, ALU.mult)
                    ii = v.tensor_tensor(b16(QN_ + (tt % 2) * 512, 512), b16(RA_, 512),
                                         b16(RB_, 512), ALU.add)
                    kn_dve[tt] = inc(ii, "dve")
                block.vector(_)

                def _(t, tt=tt):
                    ws = [slabm["wv"]]
                    if tt >= 2:
                        ws.append(("dve", v_ev[tt - 2]))
                    waits(t, ws)
                    vb = PS[:, vbank[tt] * 512:(vbank[tt] + 1) * 512]
                    ii = None
                    for c in range(NCH):
                        ii = t.matmul(vb, b16(XNT + c * 512 + tt * 128, 128),
                                      b16(R1L + c * 512, 512),
                                      start=(c == 0), stop=(c == NCH - 1))
                    v_mm[tt] = inc(ii, "pe")
                block.tensor(_)

                def _(v, tt=tt):
                    waits(v, [("pe", v_mm[tt])])
                    vb = PS[:, vbank[tt] * 512:(vbank[tt] + 1) * 512]
                    ii = v.tensor_copy(b16(KV + pb * 4096 + 2048 + tt * 512, 512), vb)
                    v_ev[tt] = inc(ii, "dve")
                block.vector(_)

                if tt >= 1:
                    emit_ktr(tt - 1)

            emit_ktr(NTIL - 1)
            nc.leave_named_scope("sc%d_A" % scn, _scope_id, False)
            _scope_id, _ = nc.enter_named_scope("sc%d_B" % scn, False)

            if "xnt" in tap_d:
                def _(sy):
                    waits(sy, [("dve", a_evac[NTIL - 1])])
                    ii = sy.dma_start(out=tap_d["xnt"][scn], in_=b16(XNT, 8192))
                    inc(ii, "st", 16)
                block.sync(_)

            if "kv" in tap_d:
                def _(sy):
                    waits(sy, [("dve", kT_ev[NTIL - 1]), ("dve", v_ev[NTIL - 1])])
                    ii = sy.dma_start(out=tap_d["kv"][scn], in_=b16(KV, 8192))
                    inc(ii, "st", 16)
                block.sync(_)

            nc.leave_named_scope("sc%d_B" % scn, _scope_id, False)
            _scope_id, _ = nc.enter_named_scope("sc%d_C0" % scn, False)
            q_mm = {}
            q_ev_act = {}
            qt_tr = {}
            qT_ev = {}
            qn_dve = {}

            def emit_wq_load(qr, gate):
                def _(sq, qr=qr, gate=gate):
                    waits(sq, gate)
                    buf = R1U if qr % 2 == 0 else R1L
                    sl = "slU" if qr % 2 == 0 else "slL"
                    ii = sq.dma_start(out=b16(buf, 8192).rearrange("p (c n) -> p c n", n=512),
                                in_=wqb_d[:, qr * 512:(qr + 1) * 512].rearrange(
                                    "(c p) n -> p c n", p=128))
                    slabm["wq%d" % qr] = (sl, inc(ii, sl, 16))
                block.sync(_)

            emit_wq_load(0, [("pe", k_mm[NTIL - 1])])
            emit_wq_load(1, [("pe", v_mm[NTIL - 1])])

            def emit_qtr(un):
                qr2, tt2 = divmod(un, NTIL)

                def _(t, un=un):
                    ws = [("dve", qn_dve[un])]
                    if un >= 2:
                        ws.append(("dve", qT_ev[un - 2]))
                    else:
                        ws.append(("dve", v_ev[NTIL - 1]))
                        ws.append(("act", k_ev_act[NTIL - 1]))
                    waits(t, ws)
                    slot = 4096 + (un % 2) * 1024
                    ii = None
                    for h in range(4):
                        ii = t.transpose(PSB[:, slot + h * 128: slot + (h + 1) * 128],
                                         b16(QN_ + (un % 4) * 512 + h * 128, 128), ident)
                    qt_tr[un] = inc(ii, "pe")
                block.tensor(_)

                def _(v, qr2=qr2, tt2=tt2, un=un):
                    waits(v, [("pe", qt_tr[un])])
                    slot = 4096 + (un % 2) * 1024
                    ii = v.tensor_copy(
                        b16(P_, 8192).rearrange("p (h t) -> p h t", t=512)
                        [:, qr2 * 4:(qr2 + 1) * 4, tt2 * 128:(tt2 + 1) * 128],
                        PSB[:, slot:slot + 512].rearrange("p (h t) -> p h t", t=128))
                    qT_ev[un] = inc(ii, "dve")
                block.vector(_)

            for qr in range(4):
                for tt in range(NTIL):
                    un = qr * NTIL + tt
                    qbank = un % 4
                    if un >= 4:
                        emit_qtr(un - 4)

                    def _(t, qr=qr, tt=tt, un=un, qbank=qbank):
                        ws = [slabm["wq%d" % qr], ("dve", a_evac[tt])]
                        if un >= 4:
                            ws.append(("act", q_ev_act[un - 4]))
                        else:
                            ws.append(("dve", kT_ev[NTIL - 1]))
                        waits(t, ws)
                        buf = R1U if qr % 2 == 0 else R1L
                        qb = PS[:, qbank * 512:(qbank + 1) * 512]
                        ii = None
                        for c in range(NCH):
                            ii = t.matmul(qb, b16(XNT + c * 512 + tt * 128, 128),
                                          b16(buf + c * 512, 512),
                                          start=(c == 0), stop=(c == NCH - 1))
                        q_mm[un] = inc(ii, "pe")
                    block.tensor(_)

                    def _(s, un=un, qbank=qbank):
                        ws = [("pe", q_mm[un])]
                        if un >= 2:
                            ws.append(("dve", qn_dve[un - 2]))
                        waits(s, ws)
                        qb = PS[:, qbank * 512:(qbank + 1) * 512]
                        ii = s.activation(b16(QSB + (un % 2) * 512, 512), qb, AF.Copy)
                        q_ev_act[un] = inc(ii, "act")
                    block.scalar(_)

                    def _(v, un=un):
                        waits(v, [("act", q_ev_act[un])])
                        v.tensor_tensor(b16(QSQ, 512), b16(QSB + (un % 2) * 512, 512),
                                        b16(QSB + (un % 2) * 512, 512), ALU.mult)
                        ii = v.tensor_reduce(
                            b32(QSS + 16, 4),
                            b16(QSQ, 512).rearrange("p (h d) -> p h d", d=128),
                            AX.X, ALU.add)
                        inc(ii, "dve")
                    block.vector(_)

                    def _(s, un=un):
                        waits(s, [("dve", cnt["dve"]), ("wg", consts)])
                        s.activation(b32(QSS + 24, 4), b32(QSS + 16, 4), AF.Ln,
                                     bias=b32(EPSC + 1, 1))
                        ii = s.activation(b32(QSS + 28, 4), b32(QSS + 24, 4), AF.Exp,
                                          scale=-0.5)
                        inc(ii, "act")
                    block.scalar(_)

                    def _(v, tt=tt, un=un):
                        ws = [("act", cnt["act"]), slabm["cos"]]
                        if un >= 4:
                            ws.append(("pe", qt_tr[un - 4]))
                        waits(v, ws)
                        ks3 = b16(QS_, 512).rearrange("p (h d) -> p h d", d=128)
                        v.tensor_tensor(ks3,
                                        b16(QSB + (un % 2) * 512, 512).rearrange("p (h d) -> p h d", d=128),
                                        b32(QSS + 28, 4).unsqueeze(2).broadcast_to([128, 4, 128]),
                                        ALU.mult)
                        cosr = b16(COSB + tt * 128, 128).unsqueeze(1).broadcast_to([128, 4, 128])
                        v.tensor_tensor(b16(RA_, 512).rearrange("p (h d) -> p h d", d=128),
                                        ks3, cosr, ALU.mult)
                        sn1 = b16(SNB + tt * 128, 64).unsqueeze(1).broadcast_to([128, 4, 64])
                        sn2 = b16(SNB + tt * 128 + 64, 64).unsqueeze(1).broadcast_to([128, 4, 64])
                        rb3 = b16(RB_, 512).rearrange("p (h d) -> p h d", d=128)
                        v.tensor_tensor(rb3[:, :, 0:64], ks3[:, :, 64:128], sn1, ALU.mult)
                        v.tensor_tensor(rb3[:, :, 64:128], ks3[:, :, 0:64], sn2, ALU.mult)
                        ii = v.tensor_tensor(b16(QN_ + (un % 4) * 512, 512), b16(RA_, 512),
                                             b16(RB_, 512), ALU.add)
                        qn_dve[un] = inc(ii, "dve")
                    block.vector(_)

                if qr == 0:
                    emit_wq_load(2, [("pe", q_mm[NTIL - 1])])
                elif qr == 1:
                    emit_wq_load(3, [("pe", q_mm[NTIL + NTIL - 1])])

            emit_qtr(4 * NTIL - 4)
            emit_qtr(4 * NTIL - 3)
            emit_qtr(4 * NTIL - 2)
            emit_qtr(4 * NTIL - 1)

            def emit_wo_load(qr, gate):
                def _(sq, qr=qr, gate=gate):
                    waits(sq, gate)
                    buf = R1U if qr % 2 == 0 else R1L
                    sl = "slU" if qr % 2 == 0 else "slL"
                    ii = sq.dma_start(out=b16(buf, 8192).rearrange("p (c n) -> p c n", n=512),
                                in_=wob_d[:, qr * 512:(qr + 1) * 512].rearrange(
                                    "(c p) n -> p c n", p=128))
                    slabm["wo%d" % qr] = (sl, inc(ii, sl, 16))
                block.sync(_)

            emit_wo_load(0, [("pe", q_mm[2 * NTIL + NTIL - 1])])
            emit_wo_load(1, [("pe", q_mm[3 * NTIL + NTIL - 1])])

            if "qt" in tap_d:
                def _(sy):
                    waits(sy, [("dve", qT_ev[15])])
                    ii = sy.dma_start(out=tap_d["qt"][scn], in_=b16(P_, 8192))
                    inc(ii, "st", 16)
                block.sync(_)

            nc.leave_named_scope("sc%d_C0" % scn, _scope_id, False)
            _scope_id, _ = nc.enter_named_scope("sc%d_C1" % scn, False)
            sp_mm = {}
            exp_g = {}
            mask_g = {}
            ot_mm = {}
            recip_g = {}
            otv_g = {}
            ngroups = NTIL * NKV * 2

            APQ = FU + 4096
            RCPW = FU + 16384
            LNT = XF[0]
            rs_mm = {}
            rcp_a = {}
            otm = {}
            otvm = {}

            for qt in range(NTIL):
                gt = u * NTIL + qt
                gcs = list(range(max(0, gt - 4), gt + 1))
                nch_ = len(gcs)
                for kvh in range(NKV):
                    for pr in range(2):
                        g_ = qt * 8 + kvh * 2 + pr
                        h0 = kvh * 4 + pr * 2
                        buf = g_ % 2
                        spb = 0 if buf == 0 else 1536

                        def _(t, qt=qt, gcs=gcs, nch_=nch_, kvh=kvh, h0=h0,
                              g_=g_, spb=spb):
                            ws = [("dve", qT_ev[(h0 // 4) * NTIL + qt]),
                                  ("dve", kT_ev[NTIL - 1]), ("dve", v_ev[NTIL - 1])]
                            if g_ >= 2:
                                ws.append(("act", exp_g[g_ - 2]))
                            else:
                                ws.append(("act", q_ev_act[15]))
                                ws.append(("dve", qT_ev[15]))
                            waits(t, ws)
                            ii = None
                            for ci, gc in enumerate(gcs):
                                pbc = (gc // NTIL) % 2
                                kts = b16(KV + pbc * 4096 + kvh * 512 + (gc % NTIL) * 128, 128)
                                for hs in range(2):
                                    h = h0 + hs
                                    ii = t.matmul(
                                        PS[:, spb + hs * 640 + ci * 128:
                                           spb + hs * 640 + (ci + 1) * 128],
                                        kts, b16(P_ + h * 512 + qt * 128, 128),
                                        start=True, stop=True, skip_group_check=True)
                            sp_mm[g_] = inc(ii, "pe")
                        block.tensor(_)

                        def _(s, nch_=nch_, g_=g_, spb=spb, h0=h0):
                            ws = [("pe", sp_mm[g_])]
                            if g_ >= 2:
                                ws.append(("dve", mask_g[g_ - 2]))
                            waits(s, ws)
                            ii = s.activation(b16(APQ + h0 * 640, 1280),
                                              PS[:, spb: spb + 1280],
                                              AF.Exp, scale=float(SCALE))
                            exp_g[g_] = inc(ii, "act")
                        block.scalar(_)

                        def _(v, gt=gt, nch_=nch_, g_=g_, h0=h0):
                            waits(v, [("act", exp_g[g_]), ("wg", consts)])
                            both_d = b16(APQ + h0 * 640, 1280).rearrange(
                                "p (h d) -> p h d", d=640)[:, :, (nch_ - 1) * 128: nch_ * 128]
                            ii = v.tensor_tensor(
                                both_d, both_d,
                                b16(MKD, 128).unsqueeze(1).broadcast_to([128, 2, 128]),
                                ALU.mult)
                            if gt >= 4:
                                both_o = b16(APQ + h0 * 640, 1280).rearrange(
                                    "p (h d) -> p h d", d=640)[:, :, 0:128]
                                ii = v.tensor_tensor(
                                    both_o, both_o,
                                    b16(MKO, 128).unsqueeze(1).broadcast_to([128, 2, 128]),
                                    ALU.mult)
                            mask_g[g_] = inc(ii, "dve")
                        block.vector(_)

                for qv in range(4):
                    def _(t, qt=qt, qv=qv, gcs=gcs, nch_=nch_):
                        ws = [("dve", mask_g[qt * 8 + qv * 2 + 1])]
                        prev = (qt, qv - 1) if qv else ((qt - 1, 3) if qt else None)
                        if prev is not None:
                            ws.append(("dve", otvm[prev]))
                        waits(t, ws)
                        ii = None
                        for hh in range(4):
                            h = qv * 4 + hh
                            for ci in range(nch_):
                                ii = t.matmul(
                                    PS[:, 3072 + hh * 128: 3072 + (hh + 1) * 128],
                                    ones128, b16(APQ + h * 640 + ci * 128, 128),
                                    start=(ci == 0), stop=(ci == nch_ - 1),
                                    skip_group_check=True)
                        rs_mm[(qt, qv)] = inc(ii, "pe")
                        for hh in range(4):
                            h = qv * 4 + hh
                            kvh = h // 4
                            for ci, gc in enumerate(gcs):
                                pbc = (gc // NTIL) % 2
                                vts = b16(KV + pbc * 4096 + 2048 + (gc % NTIL) * 512
                                          + kvh * 128, 128)
                                ii = t.matmul(
                                    PS[:, 3584 + hh * 128: 3584 + (hh + 1) * 128],
                                    vts, b16(APQ + h * 640 + ci * 128, 128),
                                    start=(ci == 0), stop=(ci == nch_ - 1),
                                    skip_group_check=True)
                        otm[(qt, qv)] = inc(ii, "pe")
                    block.tensor(_)

                    def _(s, qt=qt, qv=qv):
                        waits(s, [("pe", rs_mm[(qt, qv)])])
                        s.activation(b32(LNT, 512), PS[:, 3072: 3072 + 512], AF.Ln)
                        ii = s.activation(b16(RCPW + qv * 512, 512), b32(LNT, 512),
                                          AF.Exp, scale=-1.0)
                        rcp_a[(qt, qv)] = inc(ii, "act")
                    block.scalar(_)

                    def _(v, qt=qt, qv=qv):
                        waits(v, [("pe", otm[(qt, qv)]), ("act", rcp_a[(qt, qv)])])
                        ii = v.tensor_tensor(
                            b16(Q_, 8192).rearrange("p (h t) -> p h t", t=512)
                            [:, qv * 4:(qv + 1) * 4, qt * 128:(qt + 1) * 128],
                            PS[:, 3584: 3584 + 512].rearrange("p (h t) -> p h t", t=128),
                            b16(RCPW + qv * 512, 512).rearrange("p (h t) -> p h t", t=128),
                            ALU.mult)
                        otvm[(qt, qv)] = inc(ii, "dve")
                    block.vector(_)

            rope_last = qn_dve[15]

            if "ot" in tap_d:
                def _(sy):
                    waits(sy, [("dve", otvm[(NTIL - 1, 3)])])
                    ii = sy.dma_start(out=tap_d["ot"][scn], in_=b16(Q_, 8192))
                    inc(ii, "st", 16)
                block.sync(_)

            nc.leave_named_scope("sc%d_C1" % scn, _scope_id, False)
            _scope_id, _ = nc.enter_named_scope("sc%d_D" % scn, False)
            s_mm = {}
            s_ev = {}
            for qr in range(4):
                for tt in range(NTIL):
                    un = qr * NTIL + tt

                    def _(t, qr=qr, tt=tt, un=un):
                        ws = [slabm["wo%d" % qr],
                              ("dve", otvm[(tt, 3)])]
                        if un >= 4:
                            ws.append(("dve", s_ev[un - 4]))
                        else:
                            ws.append(("act", exp_g[ngroups - 1]))
                        waits(t, ws)
                        buf = R1U if qr % 2 == 0 else R1L
                        sb = PS[:, (un % 4) * 512:((un % 4) + 1) * 512]
                        ii = None
                        for c in range(NCH):
                            ii = t.matmul(sb, b16(Q_ + c * 512 + tt * 128, 128),
                                          b16(buf + c * 512, 512),
                                          start=(c == 0), stop=(c == NCH - 1))
                        s_mm[un] = inc(ii, "pe")
                    block.tensor(_)

                    def _(v, qr=qr, tt=tt, un=un):
                        waits(v, [("pe", s_mm[un])])
                        sb = PS[:, (un % 4) * 512:((un % 4) + 1) * 512]
                        ii = v.tensor_copy(b16(P_ + tt * 2048 + qr * 512, 512), sb)
                        s_ev[un] = inc(ii, "dve")
                    block.vector(_)

                if qr == 0:
                    emit_wo_load(2, [("pe", s_mm[NTIL - 1])])
                elif qr == 1:
                    emit_wo_load(3, [("pe", s_mm[NTIL + NTIL - 1])])

            if "ssb" in tap_d:
                def _(sy):
                    waits(sy, [("dve", s_ev[15])])
                    ii = sy.dma_start(out=tap_d["ssb"][scn], in_=b16(P_, 8192))
                    inc(ii, "st", 16)
                block.sync(_)

            nc.leave_named_scope("sc%d_D" % scn, _scope_id, False)
            _scope_id, _ = nc.enter_named_scope("sc%d_E1" % scn, False)
            gu_mm = {}
            sil_m = {}
            fu_dve = {}
            wgm = {}
            wum = {}
            wo2_done = s_mm[2 * NTIL + NTIL - 1]

            def emit_gu_load(ib):
                def _(g, ib=ib):
                    if ib < 2:
                        waits(g, [("pe", wo2_done)])
                    else:
                        waits(g, [("pe", gu_mm[ib - 2])])
                    ii = g.dma_start(out=b16(WGB[ib % 2], 2048), in_=wg_d[ib])
                    wgm[ib] = ("wgs%d" % (ib % 2), inc(ii, "wgs%d" % (ib % 2), 16))
                    ii = g.dma_start(out=b16(WUB[ib % 2], 2048), in_=wu_d[ib])
                    wum[ib] = ("wus%d" % (ib % 2), inc(ii, "wus%d" % (ib % 2), 16))
                block.gpsimd(_)

            emit_gu_load(0)
            emit_gu_load(1)

            for ib in range(NIB):
                def _(t, ib=ib):
                    ws = [wgm[ib], wum[ib]]
                    if ib >= 2:
                        ws.append(("act", sil_m[ib - 2]))
                        ws.append(("dve", fu_dve[ib - 2]))
                    else:
                        ws.append(("dve", s_ev[15]))
                    waits(t, ws)
                    gp = PS[:, (ib % 2) * 1024:(ib % 2) * 1024 + 512]
                    up = PS[:, (ib % 2) * 1024 + 512:(ib % 2) * 1024 + 1024]
                    ii = None
                    for c in range(NCH):
                        t.matmul(gp, b16(WGB[ib % 2] + c * 128, 128),
                                 b16(XNT + c * 512, 512),
                                 start=(c == 0), stop=(c == NCH - 1))
                    for c in range(NCH):
                        ii = t.matmul(up, b16(WUB[ib % 2] + c * 128, 128),
                                      b16(XNT + c * 512, 512),
                                      start=(c == 0), stop=(c == NCH - 1))
                    gu_mm[ib] = inc(ii, "pe")
                block.tensor(_)

                if ib + 2 < NIB:
                    emit_gu_load(ib + 2)

                def _(s, ib=ib):
                    waits(s, [("pe", gu_mm[ib]), ("wg", consts)])
                    gp = PS[:, (ib % 2) * 1024:(ib % 2) * 1024 + 512]
                    ii = s.activation(b16(SILT[ib % 2], 512), gp, AF.Silu,
                                      scale=b32(SGU + ib, 1))
                    sil_m[ib] = inc(ii, "act")
                block.scalar(_)

                def _(v, ib=ib):
                    waits(v, [("act", sil_m[ib])])
                    up = PS[:, (ib % 2) * 1024 + 512:(ib % 2) * 1024 + 1024]
                    v.tensor_scalar_mul(b16(UPT[ib % 2], 512), up,
                                        b32(SGU + 64 + ib, 1))
                    ii = v.tensor_tensor(b16(FU + ib * 512, 512),
                                         b16(SILT[ib % 2], 512),
                                         b16(UPT[ib % 2], 512), ALU.mult)
                    fu_dve[ib] = inc(ii, "dve")
                block.vector(_)

            if "fuse" in tap_d:
                def _(sy):
                    waits(sy, [("dve", fu_dve[NIB - 1])])
                    ii = sy.dma_start(out=tap_d["fuse"][scn], in_=b16(FU, 32768))
                    inc(ii, "st", 16)
                block.sync(_)

            nc.leave_named_scope("sc%d_E1" % scn, _scope_id, False)
            if scn + 1 < NSEQ * NSCQ:
                emit_apre(scn + 1,
                          gates=[("pe", s_mm[3 * NTIL + NTIL - 1]),
                                 ("dve", otvm[(NTIL - 1, 3)]),
                                 ("pe", a_tr[NTIL - 1])])
            _scope_id, _ = nc.enter_named_scope("sc%d_E2" % scn, False)
            dn_mm = {}
            wdm = {}
            gsc_m = {}
            f_dve = {}
            f_act = {}
            xf_mark = {}

            NWG = NIB // 2
            for hf in range(2):
                def emit_wd_load(gw, hf=hf):
                    def _(g, gw=gw, hf=hf):
                        gg = hf * NWG + gw
                        slot = gg % 4
                        if gg < 2:
                            waits(g, [("pe", s_mm[3 * NTIL + NTIL - 1])])
                        elif gg < 4:
                            waits(g, [("pe", gu_mm[NIB - 1])])
                        else:
                            pg = gg - 4
                            phf, pgw = divmod(pg, NWG)
                            waits(g, [("pe", dn_mm[(phf, 2 * pgw + 1)])])
                        ii = g.dma_start(
                            out=b16(WDB[slot], 2048).rearrange("p (g n) -> p g n", n=1024),
                            in_=wd_d[2 * gw:2 * gw + 2, :,
                                     hf * 1024:(hf + 1) * 1024].rearrange("g p n -> p g n"))
                        wdm[(hf, gw)] = ("wds%d" % slot, inc(ii, "wds%d" % slot, 16))
                    block.gpsimd(_)

                if hf == 0:
                    pend_wd = []
                    for _g in range(4):
                        emit_wd_load(_g)
                else:
                    for _g in pend_wd:
                        emit_wd_load(_g)
                    pend_wd = []

                for ts in range(2):
                    def _(sy, ts=ts, hf=hf):
                        waits(sy, [("dve", last_xf_dve[ts % 2])])
                        ii = sy.dma_start(out=b32(XF[ts % 2], 1024),
                                     in_=x_d[si, base + ts * 128: base + (ts + 1) * 128,
                                             hf * 1024:(hf + 1) * 1024])
                        xf_mark[(hf, ts)] = ("xf%d" % (ts % 2), inc(ii, "xf%d" % (ts % 2), 16))
                    block.sync(_)

                for gw in range(NWG):
                    def _(t, gw=gw, hf=hf):
                        ws = [wdm[(hf, gw)]]
                        if gw == 0:
                            if hf == 0:
                                ws.append(("dve", fu_dve[NIB - 1]))
                                ws.append(("act", sil_m[NIB - 1]))
                            else:
                                for ts2 in range(NTIL):
                                    ws.append(("dve", gsc_m[(0, ts2)]))
                        waits(t, ws)
                        ii = None
                        for ibh in range(2):
                            ib = 2 * gw + ibh
                            slot = (hf * NWG + gw) % 4
                            for ts in range(NTIL):
                                lh = b16(FU + ib * 512 + ts * 128, 128)
                                ii = t.matmul(PS[:, ts * 1024: ts * 1024 + 512], lh,
                                              b16(WDB[slot] + ibh * 1024, 512),
                                              start=(ib == 0), stop=(ib == NIB - 1))
                                ii = t.matmul(PS[:, ts * 1024 + 512: (ts + 1) * 1024], lh,
                                              b16(WDB[slot] + ibh * 1024 + 512, 512),
                                              start=(ib == 0), stop=(ib == NIB - 1))
                            dn_mm[(hf, ib)] = inc(ii, "pe")
                    block.tensor(_)

                    gg = hf * NWG + gw
                    ng = gg + 4
                    if ng < 2 * NWG:
                        nhf, ngw = divmod(ng, NWG)
                        if nhf == hf:
                            emit_wd_load(ngw)
                        else:
                            pend_wd.append(ngw)

                def _(v, hf=hf):
                    waits(v, [("pe", dn_mm[(hf, NIB - 1)]), ("wg", consts)])
                    for ts in range(NTIL):
                        ii = v.tensor_tensor(gsc_buf(ts),
                                             PS[:, ts * 1024:(ts + 1) * 1024],
                                             b32(SDN + hf * 1024, 1024), ALU.mult)
                        gsc_m[(hf, ts)] = inc(ii, "dve")
                block.vector(_)

                for ts in range(NTIL):
                    tapm = None
                    if "gsc" in tap_d:
                        def _(sy, ts=ts, hf=hf):
                            waits(sy, [("dve", gsc_m[(hf, ts)])])
                            ii = sy.dma_start(out=tap_d["gsc"][scn, hf * 4 + ts],
                                              in_=gsc_buf(ts))
                            inc(ii, "st", 16)
                        block.sync(_)
                        tapm = cnt["st"]

                    def _(s, ts=ts, hf=hf):
                        waits(s, [("dve", gsc_m[(hf, ts)])])
                        ii = s.activation(b16(GSIL[ts % 2], 1024), gsc_buf(ts),
                                          AF.Silu)
                        f_act[(hf, ts)] = inc(ii, "act")
                    block.scalar(_)

                    def _(v, ts=ts, hf=hf, tapm=tapm):
                        waits(v, [("act", f_act[(hf, ts)]), xf_mark[(hf, ts)],
                                  last_out_st[ts % 2]])
                        if tapm:
                            waits(v, [("st", tapm)])
                        v.tensor_tensor(gsc_buf(ts), b16(GSIL[ts % 2], 1024),
                                        b16(P_ + ts * 2048 + hf * 1024, 1024), ALU.mult)
                        ii = v.tensor_tensor(b32(OUTF[ts % 2], 1024), gsc_buf(ts),
                                             b32(XF[ts % 2], 1024), ALU.add)
                        f_dve[(hf, ts)] = inc(ii, "dve")
                    block.vector(_)
                    last_xf_dve[ts % 2] = f_dve[(hf, ts)]

                    def _(sy, ts=ts, hf=hf):
                        waits(sy, [("dve", f_dve[(hf, ts)])])
                        ii = sy.dma_start(out=out_d[si, base + ts * 128: base + (ts + 1) * 128,
                                               hf * 1024:(hf + 1) * 1024],
                                     in_=b32(OUTF[ts % 2], 1024))
                        last_out_st[ts % 2] = ("st%d" % (ts % 2), inc(ii, "st%d" % (ts % 2), 16))
                    block.sync(_)

                    if ts + 2 < NTIL:
                        def _(sy, ts=ts, hf=hf):
                            ts2 = ts + 2
                            waits(sy, [("dve", last_xf_dve[ts2 % 2])])
                            ii = sy.dma_start(out=b32(XF[ts2 % 2], 1024),
                                         in_=x_d[si, base + ts2 * 128:
                                                 base + (ts2 + 1) * 128,
                                                 hf * 1024:(hf + 1) * 1024])
                            xf_mark[(hf, ts2)] = ("xf%d" % (ts2 % 2), inc(ii, "xf%d" % (ts2 % 2), 16))
                        block.sync(_)

            nc.leave_named_scope("sc%d_E2" % scn, _scope_id, False)
            if sc_barrier:
                def _(t):
                    waits(t, [("dve", cnt["dve"]), ("act", cnt["act"]),
                              ("st", cnt["st"]), ("wg", cnt["wg"]), ("xl", cnt["xl"])])
                    ii = t.nop()
                    inc(ii, "pe")
                block.tensor(_)
                for engf in (block.vector, block.scalar, block.gpsimd, block.sync):
                    def _(e):
                        waits(e, [("pe", cnt["pe"])])
                    engf(_)
            bnd["pe_E2"] = dn_mm[(1, NIB - 1)]
            bnd["pe_D3"] = s_mm[3 * NTIL + NTIL - 1]
            bnd["otv_last"] = otvm[(NTIL - 1, 3)]
            bnd["fu_dve"] = fu_dve[NIB - 1]
            bnd["dve_F"] = f_dve[(1, NTIL - 1)]
            bnd["act_F"] = f_act[(1, NTIL - 1)]
            bnd["bank03_dve"] = gsc_m[(1, 1)]
            bnd["bank47_dve"] = gsc_m[(1, 3)]
            bnd["rope_dve"] = rope_last
            bnd["a_tr_last"] = a_tr[NTIL - 1]

        def _(sy):
            if cnt["st"] > 0:
                sy.wait_ge(hST, cnt["st"])
            sy.wait_ge(sems["st0"], cnt["st0"])
            sy.wait_ge(sems["st1"], cnt["st1"])
        block.sync(_)

    return nc


def _host_pack(inputs):
    wq = np.ascontiguousarray(inputs["L0_attn_wq"])
    wk = np.ascontiguousarray(inputs["L0_attn_wk"])
    wv = np.ascontiguousarray(inputs["L0_attn_wv"])
    wo = np.ascontiguousarray(inputs["L0_attn_wo"])
    wg = np.ascontiguousarray(
        inputs["L0_ffn_w_gate"].reshape(NCH, 128, NIB, 128).transpose(2, 1, 0, 3)
        .reshape(NIB, 128, HID))
    wu = np.ascontiguousarray(
        inputs["L0_ffn_w_up"].reshape(NCH, 128, NIB, 128).transpose(2, 1, 0, 3)
        .reshape(NIB, 128, HID))
    wd = np.ascontiguousarray(inputs["L0_ffn_w_down"].reshape(NIB, 128, HID))
    sg = np.ascontiguousarray(inputs["L0_ffn_s_gate"].reshape(NIB, 128).T)
    su = np.ascontiguousarray(inputs["L0_ffn_s_up"].reshape(NIB, 128).T)
    sd = np.ascontiguousarray(inputs["L0_ffn_s_down"].reshape(1, HID))

    idm = np.eye(128, dtype=BF)
    ones = np.ones((128, 128), dtype=BF)
    pp = np.arange(128)
    mkd = (pp[:, None] <= pp[None, :]).astype(BF)
    mko = (pp[:, None] > pp[None, :]).astype(BF)

    inv_freq = 1.0 / 10000.0 ** (np.arange(0, DH, 2, dtype=np.float64) / DH)
    pos = np.arange(T, dtype=np.float64)
    fr = pos[:, None] * inv_freq[None, :]
    cos = np.concatenate([np.cos(fr), np.cos(fr)], -1).astype(np.float32)
    sin = np.concatenate([np.sin(fr), np.sin(fr)], -1).astype(np.float32)
    snneg = np.concatenate([-sin[:, :DH // 2], sin[:, DH // 2:]], -1)
    cosp = np.ascontiguousarray(
        cos.reshape(T // 128, 128, DH).transpose(1, 0, 2).reshape(128, -1)).astype(BF)
    snp = np.ascontiguousarray(
        snneg.reshape(T // 128, 128, DH).transpose(1, 0, 2).reshape(128, -1)).astype(BF)

    eps = np.array([[1e-5, 1e-18]], dtype=np.float32)
    return dict(wq=wq, wk=wk, wv=wv, wo=wo, wg=wg, wu=wu, wd=wd, sg=sg, su=su,
                sd=sd, eps=eps, idm=idm, ones=ones, mkd=mkd, mko=mko, cosp=cosp,
                snp=snp)


def kernel(**inputs):
    x = np.ascontiguousarray(inputs["L0_attn_x"])
    shared = _host_pack(inputs)
    nc = build_program()
    in_maps = []
    for c in range(NCORE):
        m = dict(shared)
        m["x"] = np.ascontiguousarray(x[c * NSEQ:(c + 1) * NSEQ])
        in_maps.append(m)
    res = run_bass_kernel_spmd(nc, in_maps, core_ids=list(range(NCORE)))
    out = np.concatenate([r["out"] for r in res.results], axis=0)
    return np.ascontiguousarray(out.astype(np.float32))


# version 18 (final)
# speedup vs baseline: 1.0902x; 1.0761x over previous
import numpy as np
import ml_dtypes
import concourse.bass as bass
import concourse.mybir as mybir
from concourse.bass_utils import run_bass_kernel_spmd
from concourse.alu_op_type import AluOpType as ALU

F32 = mybir.dt.float32
BF16 = mybir.dt.bfloat16
I8 = mybir.dt.int8
AF = mybir.ActivationFunctionType
AX = mybir.AxisListType
BF = ml_dtypes.bfloat16

B, T, HID = 16, 2048, 2048
NH, NKV, DH = 16, 4, 128
INTER = 8192
NCORE = 8
NSEQ = 2
SC = 512
NTIL = 4
NSCQ = T // SC
NCH = HID // 128
NIB = INTER // 128
SCALE = 1.0 / np.sqrt(DH)

XNT = 0
R1U = 8192
R1L = 16384
KV = 24576
P_ = 32768
Q_ = 40960
FU = 49152
IDM = 81920
ONESC = 82048
MKD = 82176
MKO = 82304
COSB = 82432
SNB = 82944
A16N = 83456

XN0, XN1 = Q_ + 4096, Q_ + 6144
QSB = FU + 4096
QSQ = FU + 6144
QS_ = FU + 8192
RA_ = FU + 10240
RB_ = FU + 12288
QN_ = FU + 14336
AEX = (FU + 16384, FU + 17664)
RCP = (FU + 18944, FU + 19200)

WGB = (R1U + 0, R1U + 2048)
WUB = (R1U + 4096, R1U + 6144)
WDB = (R1L + 0, R1L + 2048, R1U + 4096, R1U + 6144)
SILT = (R1L + 4096, R1L + 4608)
UPT = (R1L + 5120, R1L + 5632)
GSIL = (R1L + 6144, R1L + 7168)

XA = (0, 2048)
GSC = 0
STAT = 4096
QSS = 4224
EPSC = 4272
LNR = 4288
SDN = 4544
SGU = 6592
XF = (6720, 7744)
OUTF = (8768, 9792)
A32N = 10816


def build_program(taps=(), sc_barrier=False, e2_serial=False):
    nc = bass.Bass("TRN2", target_bir_lowering=False)
    x_d = nc.dram_tensor("x", [NSEQ, T, HID], F32, kind="ExternalInput").ap()
    wq_d = nc.dram_tensor("wq", [HID, NH * DH], F32, kind="ExternalInput").ap()
    wk_d = nc.dram_tensor("wk", [HID, NKV * DH], F32, kind="ExternalInput").ap()
    wv_d = nc.dram_tensor("wv", [HID, NKV * DH], F32, kind="ExternalInput").ap()
    wo_d = nc.dram_tensor("wo", [NH * DH, HID], F32, kind="ExternalInput").ap()
    wg_d = nc.dram_tensor("wg", [NIB, 128, HID], I8, kind="ExternalInput").ap()
    wu_d = nc.dram_tensor("wu", [NIB, 128, HID], I8, kind="ExternalInput").ap()
    wd_d = nc.dram_tensor("wd", [NIB, 128, HID], I8, kind="ExternalInput").ap()
    sg_d = nc.dram_tensor("sg", [128, NIB], F32, kind="ExternalInput").ap()
    su_d = nc.dram_tensor("su", [128, NIB], F32, kind="ExternalInput").ap()
    sd_d = nc.dram_tensor("sd", [1, HID], F32, kind="ExternalInput").ap()
    eps_d = nc.dram_tensor("eps", [1, 2], F32, kind="ExternalInput").ap()
    idm_d = nc.dram_tensor("idm", [128, 128], BF16, kind="ExternalInput").ap()
    on_d = nc.dram_tensor("ones", [128, 128], BF16, kind="ExternalInput").ap()
    mkd_d = nc.dram_tensor("mkd", [128, 128], BF16, kind="ExternalInput").ap()
    mko_d = nc.dram_tensor("mko", [128, 128], BF16, kind="ExternalInput").ap()
    cos_d = nc.dram_tensor("cosp", [128, T], BF16, kind="ExternalInput").ap()
    sn_d = nc.dram_tensor("snp", [128, T], BF16, kind="ExternalInput").ap()
    out_d = nc.dram_tensor("out", [NSEQ, T, HID], F32, kind="ExternalOutput").ap()

    tap_d = {}
    for tp, size in (("xnt", 8192), ("kv", 8192), ("qt", 8192), ("ot", 8192),
                     ("ssb", 8192), ("fuse", 32768)):
        if tp in taps:
            tap_d[tp] = nc.dram_tensor(
                "tap_" + tp, [NSEQ * NSCQ, 128, size], BF16, kind="ExternalOutput").ap()
    if "gsc" in taps:
        tap_d["gsc"] = nc.dram_tensor(
            "tap_gsc", [NSEQ * NSCQ, 8, 128, 1024], F32, kind="ExternalOutput").ap()

    import contextlib
    _es = contextlib.ExitStack()
    _slotnames = ["slU", "slL", "cos", "wgs0", "wgs1", "wus0", "wus1",
                  "wds0", "wds1", "wds2", "wds3", "xa0", "xa1", "xf0", "xf1", "st0", "st1"]
    with (
        nc.sbuf_tensor("a16", [128, A16N], BF16) as a16_t,
        nc.sbuf_tensor("a32", [128, A32N], F32) as a32_t,
        nc.psum_tensor("ps", [128, 4096], F32) as ps_t,
        nc.semaphore("sWG") as hWG,
        nc.semaphore("sXL") as hXL,
        nc.semaphore("sST") as hST,
        nc.semaphore("sPE") as hPE,
        nc.semaphore("sACT") as hACT,
        nc.semaphore("sDVE") as hDVE,
        nc.Block() as block,
        _es,
    ):
        _slot_h = {n: _es.enter_context(nc.semaphore("s_" + n)) for n in _slotnames}
        PS = ps_t[:, :]
        PSB = PS.bitcast(BF16)
        def gsc_buf(ts):
            return b16(Q_ + ts * 1024, 1024)

        sems = {"wg": hWG, "xl": hXL, "st": hST, "pe": hPE, "act": hACT, "dve": hDVE}
        sems.update(_slot_h)
        cnt = {k: 0 for k in sems}

        def inc(instr, name, step=1):
            cnt[name] += step
            instr.then_inc(sems[name], step)
            return cnt[name]

        def waits(eng, ws):
            for name, tgt in ws:
                if tgt > 0:
                    eng.wait_ge(sems[name], tgt)

        def b16(off, n):
            return a16_t[:, off:off + n]

        def b32(off, n):
            return a32_t[:, off:off + n]

        ident = b16(IDM, 128)
        ones128 = b16(ONESC, 128)

        def _(g):
            ii = g.dma_start(out=b16(IDM, 128), in_=idm_d).then_inc(hWG, 16)
            ii = g.dma_start(out=b16(ONESC, 128), in_=on_d).then_inc(hWG, 16)
            ii = g.dma_start(out=b16(MKD, 128), in_=mkd_d).then_inc(hWG, 16)
            ii = g.dma_start(out=b16(MKO, 128), in_=mko_d).then_inc(hWG, 16)
            ii = g.dma_start(out=b32(SDN, 2048),
                        in_=sd_d[0:1, :].broadcast_to([128, HID])).then_inc(hWG, 16)
            ii = g.dma_start(out=b32(SGU, 64), in_=sg_d).then_inc(hWG, 16)
            ii = g.dma_start(out=b32(SGU + 64, 64), in_=su_d).then_inc(hWG, 16)
            ii = g.dma_start(out=b32(EPSC, 2),
                             in_=eps_d[0:1, :].broadcast_to([128, 2])).then_inc(hWG, 16)
        block.gpsimd(_)
        cnt["wg"] += 8 * 16
        consts = cnt["wg"]

        bnd = {"pe_E2": 0, "fu_dve": 0, "dve_F": 0, "act_F": 0,
               "rope_dve": 0, "pe_D3": 0, "otv_last": 0, "a_tr_last": 0,
               "bank03_dve": 0, "bank47_dve": 0}
        last_xf_dve = {0: 0, 1: 0}
        last_out_st = {0: ("st0", 0), 1: ("st1", 0)}

        apre = {}

        def emit_apre(scn2, gates=()):
            si2, u2 = divmod(scn2, NSCQ)
            base2 = u2 * SC
            marks = {"xn": [0] * NTIL, "xl": [None] * NTIL, "stats": [0] * NTIL}
            apre[scn2] = marks
            prev = apre.get(scn2 - 1)
            for tt in range(NTIL):
                xa_off = XA[tt % 2]
                xn_off = (XN0, XN1)[tt % 2]
                st = STAT + tt * 32

                def _(sy, tt=tt, xa_off=xa_off):
                    ws = []
                    if prev is not None:
                        ws.append(("act", prev["xn"][(tt % 2) + 2]))
                    if tt >= 2:
                        ws.append(("act", marks["xn"][tt - 2]))
                    waits(sy, ws)
                    ii = sy.dma_start(out=b32(xa_off, 2048),
                                 in_=x_d[si2, base2 + tt * 128: base2 + (tt + 1) * 128, :])
                    marks["xl"][tt] = ("xa%d" % (tt % 2), inc(ii, "xa%d" % (tt % 2), 16))
                block.sync(_)

                def _(v, tt=tt, xa_off=xa_off, st=st):
                    waits(v, [marks["xl"][tt]])
                    ii = None
                    for i in range(4):
                        ii = v.bn_stats(b32(st, 24)[:, i * 6:(i + 1) * 6],
                                        b32(xa_off, 2048)[:, i * 512:(i + 1) * 512])
                    inc(ii, "dve")
                    v.wait_ge(hDVE, cnt["dve"])
                    ii = v.bn_aggr(b32(st + 24, 2), b32(st, 24))
                    inc(ii, "dve")
                block.vector(_)

                def _(s, tt=tt, st=st):
                    waits(s, [("dve", cnt["dve"]), ("wg", consts)])
                    s.activation(b32(st + 26, 1), b32(st + 24, 2)[:, 1:2], AF.Ln,
                                 bias=b32(EPSC, 1))
                    ii = s.activation(b32(st + 27, 1), b32(st + 26, 1), AF.Exp,
                                      scale=-0.5)
                    inc(ii, "act")
                block.scalar(_)

                def _(v, tt=tt, st=st):
                    waits(v, [("act", cnt["act"])])
                    ii = v.scalar_tensor_tensor(b32(st + 28, 1),
                                                b32(st + 24, 2)[:, 0:1], -1.0,
                                                b32(st + 27, 1), ALU.mult, ALU.mult)
                    marks["stats"][tt] = inc(ii, "dve")
                block.vector(_)

                if tt < 2:
                    def _(s, tt=tt, xa_off=xa_off, xn_off=xn_off, st=st,
                          gates=tuple(gates)):
                        ws = [("dve", marks["stats"][tt])]
                        ws.extend(gates)
                        waits(s, ws)
                        ii = s.activation(b16(xn_off, 2048), b32(xa_off, 2048),
                                          AF.Identity,
                                          scale=b32(st + 27, 1), bias=b32(st + 28, 1))
                        marks["xn"][tt] = inc(ii, "act")
                    block.scalar(_)

        emit_apre(0)

        for scn in range(NSEQ * NSCQ):
            si, u = divmod(scn, NSCQ)
            pb = u % 2
            base = u * SC
            slabm = {}
            amk = apre[scn]
            xn_act = amk["xn"]

            def _(g):
                waits(g, [("pe", bnd["pe_E2"]), ("dve", bnd["fu_dve"])])
                ii = g.dma_start(out=b16(R1U, 8192).rearrange("p (c n) -> p c n", n=512),
                            in_=wk_d.rearrange("(c p) n -> p c n", p=128))
                slabm["wk"] = ("slU", inc(ii, "slU", 16))
                waits(g, [("dve", bnd["dve_F"]), ("pe", bnd["pe_E2"])])
                ii = g.dma_start(out=b16(R1L, 8192).rearrange("p (c n) -> p c n", n=512),
                            in_=wv_d.rearrange("(c p) n -> p c n", p=128))
                slabm["wv"] = ("slL", inc(ii, "slL", 16))
                waits(g, [("dve", bnd["rope_dve"])])
                ii = g.dma_start(out=b16(COSB, 512), in_=cos_d[:, u * 512:(u + 1) * 512])
                inc(ii, "cos", 16)
                ii = g.dma_start(out=b16(SNB, 512), in_=sn_d[:, u * 512:(u + 1) * 512])
                slabm["cos"] = ("cos", inc(ii, "cos", 16))
            block.gpsimd(_)

            _scope_id, _ = nc.enter_named_scope("sc%d_A" % scn, False)
            a_tr = [0] * NTIL
            a_evac = [0] * NTIL
            xn_act = [0] * NTIL
            xl_mark = [0] * NTIL
            k_mm = [0] * NTIL
            v_mm = [0] * NTIL
            kt_tr = [0] * NTIL
            kT_ev = [0] * NTIL
            v_ev = [0] * NTIL
            kn_dve = [0] * NTIL
            k_ev_act = [0] * NTIL
            kbank = [4, 6, 4, 6]
            vbank = [5, 7, 5, 7]

            def emit_ktr(tt):
                def _(t, tt=tt):
                    ws = [("dve", kn_dve[tt])]
                    if tt >= 2:
                        ws.append(("dve", kT_ev[tt - 2]))
                    waits(t, ws)
                    slot = (tt % 2) * 2048
                    ii = None
                    for h in range(NKV):
                        ii = t.transpose(PSB[:, slot + h * 128: slot + (h + 1) * 128],
                                         b16(QN_ + (tt % 2) * 512 + h * 128, 128), ident)
                    kt_tr[tt] = inc(ii, "pe")
                block.tensor(_)

                def _(v, tt=tt):
                    waits(v, [("pe", kt_tr[tt])])
                    slot = (tt % 2) * 2048
                    ii = v.tensor_copy(
                        b16(KV + pb * 4096, 2048).rearrange("p (h t) -> p h t", t=512)
                        [:, :, tt * 128:(tt + 1) * 128],
                        PSB[:, slot:slot + 512].rearrange("p (h t) -> p h t", t=128))
                    kT_ev[tt] = inc(ii, "dve")
                block.vector(_)

            for tt in range(NTIL):
                xn_off = (XN0, XN1, Q_ + 0, Q_ + 2048)[tt]
                if tt >= 2:
                    def _(s, tt=tt, xn_off=xn_off):
                        st = STAT + tt * 32
                        waits(s, [("dve", amk["stats"][tt]), ("dve", bnd["dve_F"])])
                        ii = s.activation(b16(xn_off, 2048), b32(XA[tt % 2], 2048),
                                          AF.Identity,
                                          scale=b32(st + 27, 1), bias=b32(st + 28, 1))
                        amk["xn"][tt] = inc(ii, "act")
                    block.scalar(_)

                def _(t, tt=tt, xn_off=xn_off):
                    ws = [("act", amk["xn"][tt]), ("wg", consts)]
                    if tt >= 2:
                        ws.append(("dve", a_evac[tt - 2]))
                        ws.append(("dve", kT_ev[tt - 2]))
                    else:
                        ws.append(("dve", bnd["bank03_dve"]))
                    waits(t, ws)
                    slot = (tt % 2) * 2048
                    ii = None
                    for c in range(NCH):
                        ii = t.transpose(PSB[:, slot + c * 128: slot + (c + 1) * 128],
                                         b16(xn_off, 2048)[:, c * 128:(c + 1) * 128],
                                         ident)
                    a_tr[tt] = inc(ii, "pe")
                block.tensor(_)

                def _(v, tt=tt):
                    waits(v, [("pe", a_tr[tt])])
                    slot = (tt % 2) * 2048
                    ii = v.tensor_copy(
                        b16(XNT, 8192).rearrange("p (c t) -> p c t", t=512)
                        [:, :, tt * 128:(tt + 1) * 128],
                        PSB[:, slot:slot + 2048].rearrange("p (c t) -> p c t", t=128))
                    a_evac[tt] = inc(ii, "dve")
                block.vector(_)

                def _(t, tt=tt):
                    ws = [("dve", a_evac[tt]), slabm["wk"]]
                    if tt >= 2:
                        ws.append(("act", k_ev_act[tt - 2]))
                    else:
                        ws.append(("dve", bnd["bank47_dve"]))
                    waits(t, ws)
                    kb = PS[:, kbank[tt] * 512:(kbank[tt] + 1) * 512]
                    ii = None
                    for c in range(NCH):
                        ii = t.matmul(kb, b16(XNT + c * 512 + tt * 128, 128),
                                      b16(R1U + c * 512, 512),
                                      start=(c == 0), stop=(c == NCH - 1))
                    k_mm[tt] = inc(ii, "pe")
                block.tensor(_)

                def _(s, tt=tt):
                    ws = [("pe", k_mm[tt])]
                    if tt >= 2:
                        ws.append(("dve", kn_dve[tt - 2]))
                    waits(s, ws)
                    kb = PS[:, kbank[tt] * 512:(kbank[tt] + 1) * 512]
                    ii = s.activation(b16(QSB + (tt % 2) * 512, 512), kb, AF.Copy)
                    k_ev_act[tt] = inc(ii, "act")
                block.scalar(_)

                def _(v, tt=tt):
                    waits(v, [("act", k_ev_act[tt])])
                    v.tensor_tensor(b16(QSQ, 512), b16(QSB + (tt % 2) * 512, 512),
                                    b16(QSB + (tt % 2) * 512, 512), ALU.mult)
                    ii = v.tensor_reduce(
                        b32(QSS, 4),
                        b16(QSQ, 512).rearrange("p (h d) -> p h d", d=128),
                        AX.X, ALU.add)
                    inc(ii, "dve")
                block.vector(_)

                def _(s, tt=tt):
                    waits(s, [("dve", cnt["dve"]), ("wg", consts)])
                    s.activation(b32(QSS + 8, 4), b32(QSS, 4), AF.Ln, bias=b32(EPSC + 1, 1))
                    ii = s.activation(b32(QSS + 12, 4), b32(QSS + 8, 4), AF.Exp,
                                      scale=-0.5)
                    inc(ii, "act")
                block.scalar(_)

                def _(v, tt=tt):
                    ws = [("act", cnt["act"]), slabm["cos"]]
                    if tt >= 2:
                        ws.append(("pe", kt_tr[tt - 2]))
                    waits(v, ws)
                    ks3 = b16(QS_, 512).rearrange("p (h d) -> p h d", d=128)
                    v.tensor_tensor(ks3,
                                    b16(QSB + (tt % 2) * 512, 512).rearrange("p (h d) -> p h d", d=128),
                                    b32(QSS + 12, 4).unsqueeze(2).broadcast_to([128, 4, 128]),
                                    ALU.mult)
                    cosr = b16(COSB + tt * 128, 128).unsqueeze(1).broadcast_to([128, 4, 128])
                    v.tensor_tensor(b16(RA_, 512).rearrange("p (h d) -> p h d", d=128),
                                    ks3, cosr, ALU.mult)
                    sn1 = b16(SNB + tt * 128, 64).unsqueeze(1).broadcast_to([128, 4, 64])
                    sn2 = b16(SNB + tt * 128 + 64, 64).unsqueeze(1).broadcast_to([128, 4, 64])
                    rb3 = b16(RB_, 512).rearrange("p (h d) -> p h d", d=128)
                    v.tensor_tensor(rb3[:, :, 0:64], ks3[:, :, 64:128], sn1, ALU.mult)
                    v.tensor_tensor(rb3[:, :, 64:128], ks3[:, :, 0:64], sn2, ALU.mult)
                    ii = v.tensor_tensor(b16(QN_ + (tt % 2) * 512, 512), b16(RA_, 512),
                                         b16(RB_, 512), ALU.add)
                    kn_dve[tt] = inc(ii, "dve")
                block.vector(_)

                def _(t, tt=tt):
                    ws = [slabm["wv"]]
                    if tt >= 2:
                        ws.append(("dve", v_ev[tt - 2]))
                    waits(t, ws)
                    vb = PS[:, vbank[tt] * 512:(vbank[tt] + 1) * 512]
                    ii = None
                    for c in range(NCH):
                        ii = t.matmul(vb, b16(XNT + c * 512 + tt * 128, 128),
                                      b16(R1L + c * 512, 512),
                                      start=(c == 0), stop=(c == NCH - 1))
                    v_mm[tt] = inc(ii, "pe")
                block.tensor(_)

                def _(v, tt=tt):
                    waits(v, [("pe", v_mm[tt])])
                    vb = PS[:, vbank[tt] * 512:(vbank[tt] + 1) * 512]
                    ii = v.tensor_copy(b16(KV + pb * 4096 + 2048 + tt * 512, 512), vb)
                    v_ev[tt] = inc(ii, "dve")
                block.vector(_)

                if tt >= 1:
                    emit_ktr(tt - 1)

            emit_ktr(NTIL - 1)
            nc.leave_named_scope("sc%d_A" % scn, _scope_id, False)
            _scope_id, _ = nc.enter_named_scope("sc%d_B" % scn, False)

            if "xnt" in tap_d:
                def _(sy):
                    waits(sy, [("dve", a_evac[NTIL - 1])])
                    ii = sy.dma_start(out=tap_d["xnt"][scn], in_=b16(XNT, 8192))
                    inc(ii, "st", 16)
                block.sync(_)

            if "kv" in tap_d:
                def _(sy):
                    waits(sy, [("dve", kT_ev[NTIL - 1]), ("dve", v_ev[NTIL - 1])])
                    ii = sy.dma_start(out=tap_d["kv"][scn], in_=b16(KV, 8192))
                    inc(ii, "st", 16)
                block.sync(_)

            nc.leave_named_scope("sc%d_B" % scn, _scope_id, False)
            _scope_id, _ = nc.enter_named_scope("sc%d_C0" % scn, False)
            q_mm = {}
            q_ev_act = {}
            qt_tr = {}
            qT_ev = {}
            qn_dve = {}

            def emit_wq_load(qr, gate):
                def _(g, qr=qr, gate=gate):
                    waits(g, gate)
                    buf = R1U if qr % 2 == 0 else R1L
                    sl = "slU" if qr % 2 == 0 else "slL"
                    ii = g.dma_start(out=b16(buf, 8192).rearrange("p (c n) -> p c n", n=512),
                                in_=wq_d[:, qr * 512:(qr + 1) * 512].rearrange(
                                    "(c p) n -> p c n", p=128))
                    slabm["wq%d" % qr] = (sl, inc(ii, sl, 16))
                block.gpsimd(_)

            emit_wq_load(0, [("pe", k_mm[NTIL - 1])])
            emit_wq_load(1, [("pe", v_mm[NTIL - 1])])

            def emit_qtr(un):
                qr2, tt2 = divmod(un, NTIL)

                def _(t, un=un):
                    ws = [("dve", qn_dve[un])]
                    if un >= 2:
                        ws.append(("dve", qT_ev[un - 2]))
                    else:
                        ws.append(("dve", v_ev[NTIL - 1]))
                        ws.append(("act", k_ev_act[NTIL - 1]))
                    waits(t, ws)
                    slot = 4096 + (un % 2) * 1024
                    ii = None
                    for h in range(4):
                        ii = t.transpose(PSB[:, slot + h * 128: slot + (h + 1) * 128],
                                         b16(QN_ + (un % 4) * 512 + h * 128, 128), ident)
                    qt_tr[un] = inc(ii, "pe")
                block.tensor(_)

                def _(v, qr2=qr2, tt2=tt2, un=un):
                    waits(v, [("pe", qt_tr[un])])
                    slot = 4096 + (un % 2) * 1024
                    ii = v.tensor_copy(
                        b16(P_, 8192).rearrange("p (h t) -> p h t", t=512)
                        [:, qr2 * 4:(qr2 + 1) * 4, tt2 * 128:(tt2 + 1) * 128],
                        PSB[:, slot:slot + 512].rearrange("p (h t) -> p h t", t=128))
                    qT_ev[un] = inc(ii, "dve")
                block.vector(_)

            for qr in range(4):
                for tt in range(NTIL):
                    un = qr * NTIL + tt
                    qbank = un % 4
                    if un >= 3:
                        emit_qtr(un - 3)

                    def _(t, qr=qr, tt=tt, un=un, qbank=qbank):
                        ws = [slabm["wq%d" % qr], ("dve", a_evac[tt])]
                        if un >= 4:
                            ws.append(("act", q_ev_act[un - 4]))
                        else:
                            ws.append(("dve", kT_ev[NTIL - 1]))
                        waits(t, ws)
                        buf = R1U if qr % 2 == 0 else R1L
                        qb = PS[:, qbank * 512:(qbank + 1) * 512]
                        ii = None
                        for c in range(NCH):
                            ii = t.matmul(qb, b16(XNT + c * 512 + tt * 128, 128),
                                          b16(buf + c * 512, 512),
                                          start=(c == 0), stop=(c == NCH - 1))
                        q_mm[un] = inc(ii, "pe")
                    block.tensor(_)

                    def _(s, un=un, qbank=qbank):
                        ws = [("pe", q_mm[un])]
                        if un >= 2:
                            ws.append(("dve", qn_dve[un - 2]))
                        waits(s, ws)
                        qb = PS[:, qbank * 512:(qbank + 1) * 512]
                        ii = s.activation(b16(QSB + (un % 2) * 512, 512), qb, AF.Copy)
                        q_ev_act[un] = inc(ii, "act")
                    block.scalar(_)

                    def _(v, un=un):
                        waits(v, [("act", q_ev_act[un])])
                        v.tensor_tensor(b16(QSQ, 512), b16(QSB + (un % 2) * 512, 512),
                                        b16(QSB + (un % 2) * 512, 512), ALU.mult)
                        ii = v.tensor_reduce(
                            b32(QSS + 16, 4),
                            b16(QSQ, 512).rearrange("p (h d) -> p h d", d=128),
                            AX.X, ALU.add)
                        inc(ii, "dve")
                    block.vector(_)

                    def _(s, un=un):
                        waits(s, [("dve", cnt["dve"]), ("wg", consts)])
                        s.activation(b32(QSS + 24, 4), b32(QSS + 16, 4), AF.Ln,
                                     bias=b32(EPSC + 1, 1))
                        ii = s.activation(b32(QSS + 28, 4), b32(QSS + 24, 4), AF.Exp,
                                          scale=-0.5)
                        inc(ii, "act")
                    block.scalar(_)

                    def _(v, tt=tt, un=un):
                        ws = [("act", cnt["act"]), slabm["cos"]]
                        if un >= 4:
                            ws.append(("pe", qt_tr[un - 4]))
                        waits(v, ws)
                        ks3 = b16(QS_, 512).rearrange("p (h d) -> p h d", d=128)
                        v.tensor_tensor(ks3,
                                        b16(QSB + (un % 2) * 512, 512).rearrange("p (h d) -> p h d", d=128),
                                        b32(QSS + 28, 4).unsqueeze(2).broadcast_to([128, 4, 128]),
                                        ALU.mult)
                        cosr = b16(COSB + tt * 128, 128).unsqueeze(1).broadcast_to([128, 4, 128])
                        v.tensor_tensor(b16(RA_, 512).rearrange("p (h d) -> p h d", d=128),
                                        ks3, cosr, ALU.mult)
                        sn1 = b16(SNB + tt * 128, 64).unsqueeze(1).broadcast_to([128, 4, 64])
                        sn2 = b16(SNB + tt * 128 + 64, 64).unsqueeze(1).broadcast_to([128, 4, 64])
                        rb3 = b16(RB_, 512).rearrange("p (h d) -> p h d", d=128)
                        v.tensor_tensor(rb3[:, :, 0:64], ks3[:, :, 64:128], sn1, ALU.mult)
                        v.tensor_tensor(rb3[:, :, 64:128], ks3[:, :, 0:64], sn2, ALU.mult)
                        ii = v.tensor_tensor(b16(QN_ + (un % 4) * 512, 512), b16(RA_, 512),
                                             b16(RB_, 512), ALU.add)
                        qn_dve[un] = inc(ii, "dve")
                    block.vector(_)

                if qr == 0:
                    emit_wq_load(2, [("pe", q_mm[NTIL - 1])])
                elif qr == 1:
                    emit_wq_load(3, [("pe", q_mm[NTIL + NTIL - 1])])

            emit_qtr(4 * NTIL - 3)
            emit_qtr(4 * NTIL - 2)
            emit_qtr(4 * NTIL - 1)

            def emit_wo_load(qr, gate):
                def _(g, qr=qr, gate=gate):
                    waits(g, gate)
                    buf = R1U if qr % 2 == 0 else R1L
                    sl = "slU" if qr % 2 == 0 else "slL"
                    ii = g.dma_start(out=b16(buf, 8192).rearrange("p (c n) -> p c n", n=512),
                                in_=wo_d[:, qr * 512:(qr + 1) * 512].rearrange(
                                    "(c p) n -> p c n", p=128))
                    slabm["wo%d" % qr] = (sl, inc(ii, sl, 16))
                block.gpsimd(_)

            emit_wo_load(0, [("pe", q_mm[2 * NTIL + NTIL - 1])])
            emit_wo_load(1, [("pe", q_mm[3 * NTIL + NTIL - 1])])

            if "qt" in tap_d:
                def _(sy):
                    waits(sy, [("dve", qT_ev[15])])
                    ii = sy.dma_start(out=tap_d["qt"][scn], in_=b16(P_, 8192))
                    inc(ii, "st", 16)
                block.sync(_)

            nc.leave_named_scope("sc%d_C0" % scn, _scope_id, False)
            _scope_id, _ = nc.enter_named_scope("sc%d_C1" % scn, False)
            sp_mm = {}
            exp_g = {}
            ot_mm = {}
            recip_g = {}
            otv_g = {}
            ngroups = NTIL * NKV * 2

            APQ = FU + 4096
            RCPW = FU + 16384
            LNT = XF[0]
            rs_mm = {}
            rcp_a = {}
            otm = {}
            otvm = {}

            for qt in range(NTIL):
                gt = u * NTIL + qt
                gcs = list(range(max(0, gt - 4), gt + 1))
                nch_ = len(gcs)
                for kvh in range(NKV):
                    for pr in range(2):
                        g_ = qt * 8 + kvh * 2 + pr
                        h0 = kvh * 4 + pr * 2
                        buf = g_ % 2
                        spb = 0 if buf == 0 else 1536

                        def _(t, qt=qt, gt=gt, gcs=gcs, nch_=nch_, kvh=kvh, h0=h0,
                              g_=g_, spb=spb):
                            ws = [("dve", qT_ev[(h0 // 4) * NTIL + qt]),
                                  ("dve", kT_ev[NTIL - 1]), ("dve", v_ev[NTIL - 1])]
                            if g_ >= 2:
                                ws.append(("act", exp_g[g_ - 2]))
                            else:
                                ws.append(("act", q_ev_act[15]))
                                ws.append(("dve", qT_ev[15]))
                            waits(t, ws)
                            ii = None
                            for ci, gc in enumerate(gcs):
                                pbc = (gc // NTIL) % 2
                                kts = b16(KV + pbc * 4096 + kvh * 512 + (gc % NTIL) * 128, 128)
                                if ci == nch_ - 1:
                                    madd = b16(MKD, 128)
                                elif ci == 0 and gt >= 4:
                                    madd = b16(MKO, 128)
                                else:
                                    madd = None
                                for hs in range(2):
                                    h = h0 + hs
                                    reg = PS[:, spb + hs * 640 + ci * 128:
                                             spb + hs * 640 + (ci + 1) * 128]
                                    ii = t.matmul(
                                        reg,
                                        kts, b16(P_ + h * 512 + qt * 128, 128),
                                        start=True, stop=madd is None,
                                        skip_group_check=True)
                                    if madd is not None:
                                        ii = t.matmul(reg, ident, madd,
                                                      start=False, stop=True,
                                                      skip_group_check=True)
                            sp_mm[g_] = inc(ii, "pe")
                        block.tensor(_)

                        def _(s, qt=qt, g_=g_, spb=spb, h0=h0):
                            ws = [("pe", sp_mm[g_])]
                            if qt == 0 and g_ >= 2:
                                ws.append(("dve", qT_ev[15]))
                            waits(s, ws)
                            ii = s.activation(b16(APQ + h0 * 640, 1280),
                                              PS[:, spb: spb + 1280],
                                              AF.Exp, scale=float(SCALE))
                            exp_g[g_] = inc(ii, "act")
                        block.scalar(_)

                for qv in range(4):
                    def _(t, qt=qt, qv=qv, gcs=gcs, nch_=nch_):
                        ws = [("act", exp_g[qt * 8 + qv * 2 + 1])]
                        prev = (qt, qv - 1) if qv else ((qt - 1, 3) if qt else None)
                        if prev is not None:
                            ws.append(("dve", otvm[prev]))
                        waits(t, ws)
                        ii = None
                        for hh in range(4):
                            h = qv * 4 + hh
                            for ci in range(nch_):
                                ii = t.matmul(
                                    PS[:, 3072 + hh * 128: 3072 + (hh + 1) * 128],
                                    ones128, b16(APQ + h * 640 + ci * 128, 128),
                                    start=(ci == 0), stop=(ci == nch_ - 1),
                                    skip_group_check=True)
                        rs_mm[(qt, qv)] = inc(ii, "pe")
                        for hh in range(4):
                            h = qv * 4 + hh
                            kvh = h // 4
                            for ci, gc in enumerate(gcs):
                                pbc = (gc // NTIL) % 2
                                vts = b16(KV + pbc * 4096 + 2048 + (gc % NTIL) * 512
                                          + kvh * 128, 128)
                                ii = t.matmul(
                                    PS[:, 3584 + hh * 128: 3584 + (hh + 1) * 128],
                                    vts, b16(APQ + h * 640 + ci * 128, 128),
                                    start=(ci == 0), stop=(ci == nch_ - 1),
                                    skip_group_check=True)
                        otm[(qt, qv)] = inc(ii, "pe")
                    block.tensor(_)

                    def _(s, qt=qt, qv=qv):
                        waits(s, [("pe", rs_mm[(qt, qv)])])
                        s.activation(b32(LNT, 512), PS[:, 3072: 3072 + 512], AF.Ln)
                        ii = s.activation(b16(RCPW + qv * 512, 512), b32(LNT, 512),
                                          AF.Exp, scale=-1.0)
                        rcp_a[(qt, qv)] = inc(ii, "act")
                    block.scalar(_)

                    def _(v, qt=qt, qv=qv):
                        waits(v, [("pe", otm[(qt, qv)]), ("act", rcp_a[(qt, qv)])])
                        ii = v.tensor_tensor(
                            b16(Q_, 8192).rearrange("p (h t) -> p h t", t=512)
                            [:, qv * 4:(qv + 1) * 4, qt * 128:(qt + 1) * 128],
                            PS[:, 3584: 3584 + 512].rearrange("p (h t) -> p h t", t=128),
                            b16(RCPW + qv * 512, 512).rearrange("p (h t) -> p h t", t=128),
                            ALU.mult)
                        otvm[(qt, qv)] = inc(ii, "dve")
                    block.vector(_)

            rope_last = qn_dve[15]

            if "ot" in tap_d:
                def _(sy):
                    waits(sy, [("dve", otvm[(NTIL - 1, 3)])])
                    ii = sy.dma_start(out=tap_d["ot"][scn], in_=b16(Q_, 8192))
                    inc(ii, "st", 16)
                block.sync(_)

            nc.leave_named_scope("sc%d_C1" % scn, _scope_id, False)
            _scope_id, _ = nc.enter_named_scope("sc%d_D" % scn, False)
            s_mm = {}
            s_ev = {}
            for qr in range(4):
                for tt in range(NTIL):
                    un = qr * NTIL + tt

                    def _(t, qr=qr, tt=tt, un=un):
                        ws = [slabm["wo%d" % qr],
                              ("dve", otvm[(tt, 3)])]
                        if un >= 4:
                            ws.append(("dve", s_ev[un - 4]))
                        else:
                            ws.append(("act", exp_g[ngroups - 1]))
                        waits(t, ws)
                        buf = R1U if qr % 2 == 0 else R1L
                        sb = PS[:, (un % 4) * 512:((un % 4) + 1) * 512]
                        ii = None
                        for c in range(NCH):
                            ii = t.matmul(sb, b16(Q_ + c * 512 + tt * 128, 128),
                                          b16(buf + c * 512, 512),
                                          start=(c == 0), stop=(c == NCH - 1))
                        s_mm[un] = inc(ii, "pe")
                    block.tensor(_)

                    def _(v, qr=qr, tt=tt, un=un):
                        waits(v, [("pe", s_mm[un])])
                        sb = PS[:, (un % 4) * 512:((un % 4) + 1) * 512]
                        ii = v.tensor_copy(b16(P_ + tt * 2048 + qr * 512, 512), sb)
                        s_ev[un] = inc(ii, "dve")
                    block.vector(_)

                if qr == 0:
                    emit_wo_load(2, [("pe", s_mm[NTIL - 1])])
                elif qr == 1:
                    emit_wo_load(3, [("pe", s_mm[NTIL + NTIL - 1])])

            if "ssb" in tap_d:
                def _(sy):
                    waits(sy, [("dve", s_ev[15])])
                    ii = sy.dma_start(out=tap_d["ssb"][scn], in_=b16(P_, 8192))
                    inc(ii, "st", 16)
                block.sync(_)

            nc.leave_named_scope("sc%d_D" % scn, _scope_id, False)
            _scope_id, _ = nc.enter_named_scope("sc%d_E1" % scn, False)
            gu_mm = {}
            sil_m = {}
            fu_dve = {}
            wgm = {}
            wum = {}
            wo2_done = s_mm[2 * NTIL + NTIL - 1]

            def emit_gu_load(ib):
                def _(g, ib=ib):
                    if ib < 2:
                        waits(g, [("pe", wo2_done)])
                    else:
                        waits(g, [("pe", gu_mm[ib - 2])])
                    ii = g.dma_start(out=b16(WGB[ib % 2], 2048), in_=wg_d[ib])
                    wgm[ib] = ("wgs%d" % (ib % 2), inc(ii, "wgs%d" % (ib % 2), 16))
                    ii = g.dma_start(out=b16(WUB[ib % 2], 2048), in_=wu_d[ib])
                    wum[ib] = ("wus%d" % (ib % 2), inc(ii, "wus%d" % (ib % 2), 16))
                block.gpsimd(_)

            emit_gu_load(0)
            emit_gu_load(1)

            for ib in range(NIB):
                def _(t, ib=ib):
                    ws = [wgm[ib], wum[ib]]
                    if ib >= 2:
                        ws.append(("act", sil_m[ib - 2]))
                        ws.append(("dve", fu_dve[ib - 2]))
                    else:
                        ws.append(("dve", s_ev[15]))
                    waits(t, ws)
                    gp = PS[:, (ib % 2) * 1024:(ib % 2) * 1024 + 512]
                    up = PS[:, (ib % 2) * 1024 + 512:(ib % 2) * 1024 + 1024]
                    ii = None
                    for c in range(NCH):
                        t.matmul(gp, b16(WGB[ib % 2] + c * 128, 128),
                                 b16(XNT + c * 512, 512),
                                 start=(c == 0), stop=(c == NCH - 1))
                    for c in range(NCH):
                        ii = t.matmul(up, b16(WUB[ib % 2] + c * 128, 128),
                                      b16(XNT + c * 512, 512),
                                      start=(c == 0), stop=(c == NCH - 1))
                    gu_mm[ib] = inc(ii, "pe")
                block.tensor(_)

                if ib + 2 < NIB:
                    emit_gu_load(ib + 2)

                def _(s, ib=ib):
                    waits(s, [("pe", gu_mm[ib]), ("wg", consts)])
                    gp = PS[:, (ib % 2) * 1024:(ib % 2) * 1024 + 512]
                    ii = s.activation(b16(SILT[ib % 2], 512), gp, AF.Silu,
                                      scale=b32(SGU + ib, 1))
                    sil_m[ib] = inc(ii, "act")
                block.scalar(_)

                def _(v, ib=ib):
                    waits(v, [("act", sil_m[ib])])
                    up = PS[:, (ib % 2) * 1024 + 512:(ib % 2) * 1024 + 1024]
                    v.tensor_scalar_mul(b16(UPT[ib % 2], 512), up,
                                        b32(SGU + 64 + ib, 1))
                    ii = v.tensor_tensor(b16(FU + ib * 512, 512),
                                         b16(SILT[ib % 2], 512),
                                         b16(UPT[ib % 2], 512), ALU.mult)
                    fu_dve[ib] = inc(ii, "dve")
                block.vector(_)

            if "fuse" in tap_d:
                def _(sy):
                    waits(sy, [("dve", fu_dve[NIB - 1])])
                    ii = sy.dma_start(out=tap_d["fuse"][scn], in_=b16(FU, 32768))
                    inc(ii, "st", 16)
                block.sync(_)

            nc.leave_named_scope("sc%d_E1" % scn, _scope_id, False)
            if scn + 1 < NSEQ * NSCQ:
                emit_apre(scn + 1,
                          gates=[("pe", s_mm[3 * NTIL + NTIL - 1]),
                                 ("dve", otvm[(NTIL - 1, 3)]),
                                 ("pe", a_tr[NTIL - 1])])
            _scope_id, _ = nc.enter_named_scope("sc%d_E2" % scn, False)
            dn_mm = {}
            wdm = {}
            gsc_m = {}
            f_dve = {}
            f_act = {}
            xf_mark = {}

            NWG = NIB // 2
            for hf in range(2):
                def emit_wd_load(gw, hf=hf):
                    def _(g, gw=gw, hf=hf):
                        gg = hf * NWG + gw
                        slot = gg % 4
                        if gg < 2:
                            waits(g, [("pe", s_mm[3 * NTIL + NTIL - 1])])
                        elif gg < 4:
                            waits(g, [("pe", gu_mm[NIB - 1])])
                        else:
                            pg = gg - 4
                            phf, pgw = divmod(pg, NWG)
                            waits(g, [("pe", dn_mm[(phf, 2 * pgw + 1)])])
                        ii = g.dma_start(
                            out=b16(WDB[slot], 2048).rearrange("p (g n) -> p g n", n=1024),
                            in_=wd_d[2 * gw:2 * gw + 2, :,
                                     hf * 1024:(hf + 1) * 1024].rearrange("g p n -> p g n"))
                        wdm[(hf, gw)] = ("wds%d" % slot, inc(ii, "wds%d" % slot, 16))
                    block.gpsimd(_)

                if hf == 0:
                    pend_wd = []
                    for _g in range(4):
                        emit_wd_load(_g)
                else:
                    for _g in pend_wd:
                        emit_wd_load(_g)
                    pend_wd = []

                for ts in range(2):
                    def _(sy, ts=ts, hf=hf):
                        waits(sy, [("dve", last_xf_dve[ts % 2])])
                        ii = sy.dma_start(out=b32(XF[ts % 2], 1024),
                                     in_=x_d[si, base + ts * 128: base + (ts + 1) * 128,
                                             hf * 1024:(hf + 1) * 1024])
                        xf_mark[(hf, ts)] = ("xf%d" % (ts % 2), inc(ii, "xf%d" % (ts % 2), 16))
                    block.sync(_)

                for gw in range(NWG):
                    def _(t, gw=gw, hf=hf):
                        ws = [wdm[(hf, gw)]]
                        if gw == 0:
                            if hf == 0:
                                ws.append(("dve", fu_dve[NIB - 1]))
                                ws.append(("act", sil_m[NIB - 1]))
                            else:
                                for ts2 in range(NTIL):
                                    ws.append(("dve", gsc_m[(0, ts2)]))
                        waits(t, ws)
                        ii = None
                        for ibh in range(2):
                            ib = 2 * gw + ibh
                            slot = (hf * NWG + gw) % 4
                            for ts in range(NTIL):
                                lh = b16(FU + ib * 512 + ts * 128, 128)
                                ii = t.matmul(PS[:, ts * 1024: ts * 1024 + 512], lh,
                                              b16(WDB[slot] + ibh * 1024, 512),
                                              start=(ib == 0), stop=(ib == NIB - 1))
                                ii = t.matmul(PS[:, ts * 1024 + 512: (ts + 1) * 1024], lh,
                                              b16(WDB[slot] + ibh * 1024 + 512, 512),
                                              start=(ib == 0), stop=(ib == NIB - 1))
                            dn_mm[(hf, ib)] = inc(ii, "pe")
                    block.tensor(_)

                    gg = hf * NWG + gw
                    ng = gg + 4
                    if ng < 2 * NWG:
                        nhf, ngw = divmod(ng, NWG)
                        if nhf == hf:
                            emit_wd_load(ngw)
                        else:
                            pend_wd.append(ngw)

                def _(v, hf=hf):
                    waits(v, [("pe", dn_mm[(hf, NIB - 1)]), ("wg", consts)])
                    for ts in range(NTIL):
                        ii = v.tensor_tensor(gsc_buf(ts),
                                             PS[:, ts * 1024:(ts + 1) * 1024],
                                             b32(SDN + hf * 1024, 1024), ALU.mult)
                        gsc_m[(hf, ts)] = inc(ii, "dve")
                block.vector(_)

                for ts in range(NTIL):
                    tapm = None
                    if "gsc" in tap_d:
                        def _(sy, ts=ts, hf=hf):
                            waits(sy, [("dve", gsc_m[(hf, ts)])])
                            ii = sy.dma_start(out=tap_d["gsc"][scn, hf * 4 + ts],
                                              in_=gsc_buf(ts))
                            inc(ii, "st", 16)
                        block.sync(_)
                        tapm = cnt["st"]

                    def _(s, ts=ts, hf=hf):
                        waits(s, [("dve", gsc_m[(hf, ts)])])
                        ii = s.activation(b16(GSIL[ts % 2], 1024), gsc_buf(ts),
                                          AF.Silu)
                        f_act[(hf, ts)] = inc(ii, "act")
                    block.scalar(_)

                    def _(v, ts=ts, hf=hf, tapm=tapm):
                        waits(v, [("act", f_act[(hf, ts)]), xf_mark[(hf, ts)],
                                  last_out_st[ts % 2]])
                        if tapm:
                            waits(v, [("st", tapm)])
                        v.tensor_tensor(gsc_buf(ts), b16(GSIL[ts % 2], 1024),
                                        b16(P_ + ts * 2048 + hf * 1024, 1024), ALU.mult)
                        ii = v.tensor_tensor(b32(OUTF[ts % 2], 1024), gsc_buf(ts),
                                             b32(XF[ts % 2], 1024), ALU.add)
                        f_dve[(hf, ts)] = inc(ii, "dve")
                    block.vector(_)
                    last_xf_dve[ts % 2] = f_dve[(hf, ts)]

                    def _(sy, ts=ts, hf=hf):
                        waits(sy, [("dve", f_dve[(hf, ts)])])
                        ii = sy.dma_start(out=out_d[si, base + ts * 128: base + (ts + 1) * 128,
                                               hf * 1024:(hf + 1) * 1024],
                                     in_=b32(OUTF[ts % 2], 1024))
                        last_out_st[ts % 2] = ("st%d" % (ts % 2), inc(ii, "st%d" % (ts % 2), 16))
                    block.sync(_)

                    if ts + 2 < NTIL:
                        def _(sy, ts=ts, hf=hf):
                            ts2 = ts + 2
                            waits(sy, [("dve", last_xf_dve[ts2 % 2])])
                            ii = sy.dma_start(out=b32(XF[ts2 % 2], 1024),
                                         in_=x_d[si, base + ts2 * 128:
                                                 base + (ts2 + 1) * 128,
                                                 hf * 1024:(hf + 1) * 1024])
                            xf_mark[(hf, ts2)] = ("xf%d" % (ts2 % 2), inc(ii, "xf%d" % (ts2 % 2), 16))
                        block.sync(_)

            nc.leave_named_scope("sc%d_E2" % scn, _scope_id, False)
            if sc_barrier:
                def _(t):
                    waits(t, [("dve", cnt["dve"]), ("act", cnt["act"]),
                              ("st", cnt["st"]), ("wg", cnt["wg"]), ("xl", cnt["xl"])])
                    ii = t.nop()
                    inc(ii, "pe")
                block.tensor(_)
                for engf in (block.vector, block.scalar, block.gpsimd, block.sync):
                    def _(e):
                        waits(e, [("pe", cnt["pe"])])
                    engf(_)
            bnd["pe_E2"] = dn_mm[(1, NIB - 1)]
            bnd["pe_D3"] = s_mm[3 * NTIL + NTIL - 1]
            bnd["otv_last"] = otvm[(NTIL - 1, 3)]
            bnd["fu_dve"] = fu_dve[NIB - 1]
            bnd["dve_F"] = f_dve[(1, NTIL - 1)]
            bnd["act_F"] = f_act[(1, NTIL - 1)]
            bnd["bank03_dve"] = gsc_m[(1, 1)]
            bnd["bank47_dve"] = gsc_m[(1, 3)]
            bnd["rope_dve"] = rope_last
            bnd["a_tr_last"] = a_tr[NTIL - 1]

        def _(sy):
            if cnt["st"] > 0:
                sy.wait_ge(hST, cnt["st"])
            sy.wait_ge(sems["st0"], cnt["st0"])
            sy.wait_ge(sems["st1"], cnt["st1"])
        block.sync(_)

    return nc


def _host_pack(inputs):
    wq = np.ascontiguousarray(inputs["L0_attn_wq"])
    wk = np.ascontiguousarray(inputs["L0_attn_wk"])
    wv = np.ascontiguousarray(inputs["L0_attn_wv"])
    wo = np.ascontiguousarray(inputs["L0_attn_wo"])
    wg = np.ascontiguousarray(
        inputs["L0_ffn_w_gate"].reshape(NCH, 128, NIB, 128).transpose(2, 1, 0, 3)
        .reshape(NIB, 128, HID))
    wu = np.ascontiguousarray(
        inputs["L0_ffn_w_up"].reshape(NCH, 128, NIB, 128).transpose(2, 1, 0, 3)
        .reshape(NIB, 128, HID))
    wd = np.ascontiguousarray(inputs["L0_ffn_w_down"].reshape(NIB, 128, HID))
    sg = np.ascontiguousarray(inputs["L0_ffn_s_gate"].reshape(NIB, 128).T)
    su = np.ascontiguousarray(inputs["L0_ffn_s_up"].reshape(NIB, 128).T)
    sd = np.ascontiguousarray(inputs["L0_ffn_s_down"].reshape(1, HID))

    idm = np.eye(128, dtype=BF)
    ones = np.ones((128, 128), dtype=BF)
    pp = np.arange(128)
    mkd = (-340.0 * (pp[:, None] > pp[None, :])).astype(BF)
    mko = (-340.0 * (pp[:, None] <= pp[None, :])).astype(BF)

    inv_freq = 1.0 / 10000.0 ** (np.arange(0, DH, 2, dtype=np.float64) / DH)
    pos = np.arange(T, dtype=np.float64)
    fr = pos[:, None] * inv_freq[None, :]
    cos = np.concatenate([np.cos(fr), np.cos(fr)], -1).astype(np.float32)
    sin = np.concatenate([np.sin(fr), np.sin(fr)], -1).astype(np.float32)
    snneg = np.concatenate([-sin[:, :DH // 2], sin[:, DH // 2:]], -1)
    cosp = np.ascontiguousarray(
        cos.reshape(T // 128, 128, DH).transpose(1, 0, 2).reshape(128, -1)).astype(BF)
    snp = np.ascontiguousarray(
        snneg.reshape(T // 128, 128, DH).transpose(1, 0, 2).reshape(128, -1)).astype(BF)

    eps = np.array([[1e-5, 1e-18]], dtype=np.float32)
    return dict(wq=wq, wk=wk, wv=wv, wo=wo, wg=wg, wu=wu, wd=wd, sg=sg, su=su,
                sd=sd, eps=eps, idm=idm, ones=ones, mkd=mkd, mko=mko, cosp=cosp,
                snp=snp)


def kernel(**inputs):
    x = np.ascontiguousarray(inputs["L0_attn_x"])
    shared = _host_pack(inputs)
    nc = build_program()
    in_maps = []
    for c in range(NCORE):
        m = dict(shared)
        m["x"] = np.ascontiguousarray(x[c * NSEQ:(c + 1) * NSEQ])
        in_maps.append(m)
    res = run_bass_kernel_spmd(nc, in_maps, core_ids=list(range(NCORE)))
    out = np.concatenate([r["out"] for r in res.results], axis=0)
    return np.ascontiguousarray(out.astype(np.float32))
